# version 04
# baseline (speedup 1.0000x reference)
import math
import contextlib
import numpy as np
import concourse.bass as bass
import concourse.mybir as mybir
from concourse.bass_utils import run_bass_kernel_spmd

F32 = mybir.dt.float32
BF16 = mybir.dt.bfloat16
AF = mybir.ActivationFunctionType
ALU = mybir.AluOpType
AX = mybir.AxisListType.X

NCORES = 8
B, S, D = 16, 2048, 2048
BPC = B // NCORES
CH = 512
NCH = S // CH
NTT = CH // 128
NDT = D // 128
HA, KVA, DA = 8, 2, 256
HB, DB = 16, 128
HID = 8192
NHT = HID // 128
EPS_LN = 1e-5
EPS_HEAD = 1e-6
EPS_RMS = 1.1920929e-07
THR_HI = 0.5009765625
INVSQ_DB = 1.0 / math.sqrt(128.0)

N_SUB = 4
DEBUG_A = False
DEBUG_TRUNC = False


class Walker:
    def __init__(self, sems):
        self.sems = sems
        self.eng_name = None
        self.eng = None
        self.cnt = None

    def begin(self, eng_name, eng):
        self.eng_name = eng_name
        self.eng = eng
        self.cnt = {k: 0 for k in self.sems}

    CHAIN = {"vector": "dv", "scalar": "ac", "gpsimd": "pl"}

    def do(self, eng_name, fn, waits=(), inc=None):
        own = self.CHAIN.get(eng_name)
        if own is not None:
            waits = list(waits) + [(own, self.cnt[own])]
            if inc is None:
                inc = (own, 1)
        if eng_name == self.eng_name:
            seen = {}
            for s, v in waits:
                if v > 0 and seen.get(s, -1) < v:
                    seen[s] = v
            for s, v in seen.items():
                self.eng.wait_ge(self.sems[s], v)
            last = fn(self.eng)
            if inc is not None:
                assert last is not None, f"task on {eng_name} returned no instruction"
                last.then_inc(self.sems[inc[0]], inc[1])
        if inc is not None:
            self.cnt[inc[0]] += inc[1]

    def at(self, s):
        return self.cnt[s]


class Slot:

    def __init__(self):
        self.ready = {}
        self.free = {}

    def reset(self):
        self.ready = {}
        self.free = {}

    def set_ready(self, sem, val):
        self.ready = {sem: val}

    def consume(self, sem, val):
        self.free[sem] = max(self.free.get(sem, 0), val)

    def r(self):
        return list(self.ready.items())

    def f(self):
        return list(self.free.items())


def build(n_sub=N_SUB):
    nc = bass.Bass("TRN2", target_bir_lowering=False)
    TOK = BPC * S

    xin = nc.dram_tensor("xin", [TOK, D], F32, kind="ExternalInput").ap()
    out = nc.dram_tensor("out", [TOK, D], F32, kind="ExternalOutput").ap()
    dbg = nc.dram_tensor("dbg", [128, 16384], F32, kind="ExternalOutput").ap()
    wq_a = nc.dram_tensor("wq_a", [2048, 2048], F32, kind="ExternalInput").ap()
    wk_a = nc.dram_tensor("wk_a", [512, 2048], F32, kind="ExternalInput").ap()
    wv_a = nc.dram_tensor("wv_a", [512, 2048], F32, kind="ExternalInput").ap()
    wo_a = nc.dram_tensor("wo_a", [2048, 2048], F32, kind="ExternalInput").ap()
    wq_b = nc.dram_tensor("wq_b", [2048, 2048], F32, kind="ExternalInput").ap()
    wk_b = nc.dram_tensor("wk_b", [2048, 2048], F32, kind="ExternalInput").ap()
    wv_b = nc.dram_tensor("wv_b", [2048, 2048], F32, kind="ExternalInput").ap()
    wo_b = nc.dram_tensor("wo_b", [32768, 128], F32, kind="ExternalInput").ap()
    wgu = nc.dram_tensor("wgu", [16384, 2048], F32, kind="ExternalInput").ap()
    wpr = nc.dram_tensor("wpr", [2048, 8192], F32, kind="ExternalInput").ap()
    c_identb = nc.dram_tensor("c_identb", [128, 128], BF16, kind="ExternalInput").ap()
    c_identf = nc.dram_tensor("c_identf", [128, 128], F32, kind="ExternalInput").ap()
    c_maskle = nc.dram_tensor("c_maskle", [128, 128], BF16, kind="ExternalInput").ap()
    c_maskgt = nc.dram_tensor("c_maskgt", [128, 128], BF16, kind="ExternalInput").ap()
    c_ones = nc.dram_tensor("c_ones", [128, 128], BF16, kind="ExternalInput").ap()
    c_cos = nc.dram_tensor("c_cos", [S, 128], F32, kind="ExternalInput").ap()
    c_sin = nc.dram_tensor("c_sin", [S, 128], F32, kind="ExternalInput").ap()
    c_e30le = nc.dram_tensor("c_e30le", [128, 128], BF16, kind="ExternalInput").ap()
    c_e30gt = nc.dram_tensor("c_e30gt", [128, 128], BF16, kind="ExternalInput").ap()
    c_e30c = nc.dram_tensor("c_e30c", [128, 128], BF16, kind="ExternalInput").ap()

    nat_aq = nc.dram_tensor("nat_aq", [2048, 2048], BF16, kind="Internal").ap()
    nat_ak = nc.dram_tensor("nat_ak", [512, 2048], BF16, kind="Internal").ap()
    nat_av = nc.dram_tensor("nat_av", [512, 2048], BF16, kind="Internal").ap()
    nat_ao = nc.dram_tensor("nat_ao", [2048, 2048], BF16, kind="Internal").ap()
    nat_bq = nc.dram_tensor("nat_bq", [2048, 2048], BF16, kind="Internal").ap()
    nat_bk = nc.dram_tensor("nat_bk", [2048, 2048], BF16, kind="Internal").ap()
    nat_bv = nc.dram_tensor("nat_bv", [2048, 2048], BF16, kind="Internal").ap()
    nat_bo = nc.dram_tensor("nat_bo", [32768, 128], BF16, kind="Internal").ap()
    nat_gu = nc.dram_tensor("nat_gu", [16384, 2048], BF16, kind="Internal").ap()
    nat_pr = nc.dram_tensor("nat_pr", [2048, 8192], BF16, kind="Internal").ap()
    f_awq = nc.dram_tensor("f_awq", [4, 16, 128, 512], BF16, kind="Internal").ap()
    f_awk = nc.dram_tensor("f_awk", [1, 16, 128, 512], BF16, kind="Internal").ap()
    f_awv = nc.dram_tensor("f_awv", [1, 16, 128, 512], BF16, kind="Internal").ap()
    f_awo = nc.dram_tensor("f_awo", [16, 4, 128, 512], BF16, kind="Internal").ap()
    f_bwq = nc.dram_tensor("f_bwq", [4, 16, 128, 512], BF16, kind="Internal").ap()
    f_bwk = nc.dram_tensor("f_bwk", [4, 16, 128, 512], BF16, kind="Internal").ap()
    f_bwv = nc.dram_tensor("f_bwv", [4, 16, 128, 512], BF16, kind="Internal").ap()
    f_bwo = nc.dram_tensor("f_bwo", [16, 4, 128, 512], BF16, kind="Internal").ap()
    f_gu = nc.dram_tensor("f_gu", [32, 16, 128, 512], BF16, kind="Internal").ap()
    f_pr = nc.dram_tensor("f_pr", [64, 4, 128, 512], BF16, kind="Internal").ap()
    ks_scr = nc.dram_tensor("ks_scr", [16, 4, 128, 512], BF16, kind="Internal").ap()
    vs_scr = nc.dram_tensor("vs_scr", [16, 16, 128, 128], BF16, kind="Internal").ap()

    ctx = contextlib.ExitStack()

    def sb(name, cols, dt):
        return ctx.enter_context(nc.sbuf_tensor(name, [128, cols], dt))

    xnT = sb("xnT", 16 * 512, BF16)
    xt = [sb(f"xt{i}", 2048, F32) for i in range(2)]
    xnb = sb("xnb", 2048, BF16)
    sqdump = sb("sqdump", 2048, F32)
    outsb = [sb(f"outsb{i}", 512, F32) for i in range(2)]
    xrd = [sb(f"xrd{i}", 512, F32) for i in range(2)]
    identb = sb("identb", 128, BF16)
    identf = sb("identf", 128, F32)
    maskle = sb("maskle", 128, BF16)
    maskgt = sb("maskgt", 128, BF16)
    onesb = sb("onesb", 128, BF16)
    ropeC = sb("ropeC", 128, F32)
    ropeS = sb("ropeS", 128, F32)
    small = sb("small", 64, F32)
    biasneg = sb("biasneg", 1, F32)
    invrms = sb("invrms", 8, F32)
    wsA = sb("wsA", 16 * 512, BF16)
    wsB = sb("wsB", 16 * 512, BF16)
    tmpf1 = sb("tmpf1", 2048, F32)
    tmpb = [sb(f"tmpb{i}", 512, BF16) for i in range(4)]
    e30le = sb("e30le", 128, BF16)
    e30gt = sb("e30gt", 128, BF16)
    c_e30c_sb = sb("c_e30c_sb", 128, BF16)
    sall = sb("sall", 514, F32)
    srowb = sb("srowb", 514, BF16)
    harena = sb("harena", 32768, BF16)
    aux2 = sb("aux2", 8192, BF16)
    aux3 = sb("aux3", 8192, BF16)
    qf32 = sb("qf32", 2048, F32)
    qsm = sb("qsm", 96, F32)

    rem = nc.sbuf_bytes_remaining
    assert rem >= 0, f"SBUF over budget by {-rem} bytes"

    def H(a, b):
        return harena[:, a:b]

    p512 = [ctx.enter_context(nc.psum_tensor(f"p512_{i}", [128, 512], F32)) for i in range(4)]
    p1024 = ctx.enter_context(nc.psum_tensor("p1024", [128, 1024], F32))
    pt = [ctx.enter_context(nc.psum_tensor(f"pt{i}", [128, 512], BF16)) for i in range(2)]

    semnames = ("ld", "sd", "pe", "dv", "ac", "gp", "gc", "gr", "gk", "gkc", "pl",
                "lA", "lB", "lx0", "lx1", "lr0", "lr1", "lt",
                "ko0", "ko1", "vo0", "vo1", "tt0", "tt1", "tt2", "tt3",
                "sd0", "sd1", "sd2", "sd3",
                "gq0", "gq1", "gs0", "gs1", "go0", "go1")
    sems = {s: ctx.enter_context(nc.semaphore(s)) for s in semnames}
    w = Walker(sems)

    S_xt = [Slot(), Slot()]
    S_xnb = Slot()
    S_xrd = [Slot(), Slot()]
    S_out = [Slot(), Slot()]
    S_ws = [Slot(), Slot()]
    S_tmpf = [Slot(), Slot()]
    S_tmpb = [Slot() for _ in range(4)]
    S_p = [Slot() for _ in range(4)]
    S_p10 = [Slot(), Slot()]
    S_pt = [Slot(), Slot()]
    S_xnT = Slot()
    S_qt = Slot()
    S_ot = Slot()
    S_har = Slot()
    S_kold = [Slot(), Slot()]
    S_vold = [Slot(), Slot()]
    ALL_SLOTS = ([S_xnb] + S_xt + S_xrd + S_out + S_ws + S_tmpf + S_tmpb + S_p + S_p10
                 + S_pt + [S_xnT, S_qt, S_ot, S_har] + S_kold + S_vold)

    state = {}

    def walk():
        for sl in ALL_SLOTS:
            sl.reset()
        state.clear()

        for dstt, srct in ((identb, c_identb), (identf, c_identf), (maskle, c_maskle),
                           (maskgt, c_maskgt), (onesb, c_ones), (e30le, c_e30le),
                           (e30gt, c_e30gt), (c_e30c_sb, c_e30c)):
            w.do("sync", lambda e, d=dstt, s=srct: e.dma_start(out=d[:, :], in_=s[:, :]),
                 inc=("ld", 16))
        w.do("vector", lambda e: e.memset(biasneg[:, :], -20.0), inc=("dv", 1))
        consts_ld = w.at("ld")
        state["consts_ld"] = consts_ld

        for src, dst in ((wq_a, nat_aq), (wk_a, nat_ak), (wv_a, nat_av), (wo_a, nat_ao),
                         (wq_b, nat_bq), (wk_b, nat_bk), (wv_b, nat_bv), (wo_b, nat_bo)):
            nr = src.shape[0]
            step = 512
            for r in range(0, nr, step):
                rr = min(step, nr - r)
                w.do("gpsimd", lambda e, s=src, d=dst, r=r, rr=rr: e.dma_start(
                    out=d[r:r + rr, :], in_=s[r:r + rr, :]), inc=("gc", 16))

        cast_gate = [("gc", w.at("gc"))]
        qslot_free = [{}, {}]
        qstore_free = [{}, {}]

        QQ = 20512

        def quant_weight(src, nat, nrows, C):
            tiles = [(rt, cc) for rt in range(nrows // 128) for cc in range(0, C, 2048)]
            for it, (rt, cc) in enumerate(tiles):
                CW = min(2048, C - cc)
                G = CW // 64
                sl = it % 2
                qbuf = H(QQ + sl * 2048, QQ + sl * 2048 + CW)
                pbuf = H(QQ + 4096 + sl * 2048, QQ + 4096 + sl * 2048 + CW)
                mbuf = H(QQ + 8192, QQ + 8192 + CW)
                scl = H(30816, 30816 + G)
                redf = qsm[:, 0:G]
                thr = qsm[:, 32:32 + G]
                w.do("gpsimd", lambda e, s=src, rt=rt, cc=cc, CW=CW, qb=qbuf: e.dma_start(
                    out=qb, in_=s[rt * 128:(rt + 1) * 128, cc:cc + CW]),
                    waits=list(qslot_free[sl].items()), inc=(f"gq{sl}", 16))
                ldv = (f"gq{sl}", w.at(f"gq{sl}"))
                qs = qbuf.rearrange("p (g k) -> p g k", k=64)
                pbs = pbuf.rearrange("p (g k) -> p g k", k=64)
                mbs = mbuf.rearrange("p (g k) -> p g k", k=64)
                w.do("vector", lambda e, rf=redf, qs=qs: e.tensor_reduce(
                    rf, qs, AX, ALU.add, apply_absolute_value=True),
                    waits=[ldv], inc=("dv", 1))
                rdv = w.at("dv")
                w.do("gpsimd", lambda e, sc=scl, rf=redf: e.tensor_scalar(
                    sc, rf, 1.0 / 64.0, 1e-8, ALU.mult, ALU.max),
                    waits=[("dv", rdv)])
                w.do("gpsimd", lambda e, th=thr, sc=scl: e.tensor_single_scalar(
                    th, sc, THR_HI, ALU.mult))
                HUGE = 2.0 ** 60
                w.do("gpsimd", lambda e, pbs=pbs, qs=qs, th=thr, G=G: e.tensor_tensor(
                    pbs, qs, th[:, :, None].broadcast_to([128, G, 64]), ALU.subtract),
                    waits=list(qstore_free[sl].items()))
                w.do("gpsimd", lambda e, pb=pbuf: e.tensor_scalar(
                    pb, pb, HUGE, 1.0, ALU.mult, ALU.min))
                w.do("gpsimd", lambda e, pb=pbuf: e.tensor_single_scalar(
                    pb, pb, 0.0, ALU.max))
                w.do("gpsimd", lambda e, mbs=mbs, qs=qs, th=thr, G=G: e.tensor_tensor(
                    mbs, qs, th[:, :, None].broadcast_to([128, G, 64]), ALU.add))
                w.do("gpsimd", lambda e, mb=mbuf: e.tensor_scalar(
                    mb, mb, -HUGE, 0.0, ALU.mult, ALU.max))
                w.do("gpsimd", lambda e, mb=mbuf: e.tensor_single_scalar(
                    mb, mb, 1.0, ALU.min))
                w.do("gpsimd", lambda e, pb=pbuf, mb=mbuf: e.tensor_tensor(
                    pb, pb, mb, ALU.subtract))
                w.do("gpsimd", lambda e, pbs=pbs, sc=scl, G=G: e.tensor_tensor(
                    pbs, pbs, sc[:, :, None].broadcast_to([128, G, 64]), ALU.mult))
                plv = w.at("pl")
                w.do("gpsimd", lambda e, n=nat, rt=rt, cc=cc, CW=CW, pb=pbuf: e.dma_start(
                    out=n[rt * 128:(rt + 1) * 128, cc:cc + CW], in_=pb),
                    inc=(f"gs{sl}", 16))
                qslot_free[sl] = {"pl": plv, "dv": rdv}
                qstore_free[sl] = {f"gs{sl}": w.at(f"gs{sl}")}
        quant_weight(wgu, nat_gu, 16384, 2048)
        quant_weight(wpr, nat_pr, 2048, 8192)
        pre_gates = cast_gate + [("gs0", w.at("gs0")), ("gs1", w.at("gs1"))]
        state["pre_gates"] = pre_gates
        state["quant_pl"] = w.at("pl")

        tstate = [Slot() for _ in range(4)]
        for _ts in tstate:
            _ts.free = {"dv": w.at("dv")}
        tcnt = [0]

        def transform(natsrc, r0, RB, c0, dst_ap, gates=None):
            i = tcnt[0] % 4
            tcnt[0] += 1
            sem = f"tt{i}"
            buf = aux2[:, i * 2048: i * 2048 + RB]
            w.do("sync", lambda e, b=buf: e.dma_start(
                out=b, in_=natsrc[r0:r0 + RB, c0:c0 + 128], transpose=True),
                waits=(gates if gates is not None else pre_gates) + tstate[i].f(),
                inc=(sem, 16))
            lv = w.at(sem)
            if len(dst_ap.shape) == 3:
                bsrc = buf.rearrange("p (e f) -> p e f", f=dst_ap.shape[2])
            else:
                bsrc = buf
            ssem = f"sd{i}"
            w.do("scalar", lambda e, b=bsrc, d=dst_ap: e.dma_start(out=d, in_=b),
                 waits=[(sem, lv)], inc=(ssem, 16))
            tstate[i].free = {ssem: w.at(ssem)}

        for dt in range(16):
            transform(nat_aq, 0, 2048, dt * 128, f_awq[:, dt].rearrange("e p f -> p e f"), gates=cast_gate)
        for dt in range(16):
            transform(nat_ak, 0, 512, dt * 128, f_awk[0, dt], gates=cast_gate)
            transform(nat_av, 0, 512, dt * 128, f_awv[0, dt], gates=cast_gate)
        for eb in range(16):
            transform(nat_ao, 0, 2048, eb * 128, f_awo[eb].rearrange("s p f -> p s f"), gates=cast_gate)
        for dt in range(16):
            transform(nat_bq, 0, 2048, dt * 128, f_bwq[:, dt].rearrange("e p f -> p e f"), gates=cast_gate)
            transform(nat_bk, 0, 2048, dt * 128, f_bwk[:, dt].rearrange("e p f -> p e f"), gates=cast_gate)
            transform(nat_bv, 0, 2048, dt * 128, f_bwv[:, dt].rearrange("e p f -> p e f"), gates=cast_gate)
        for h in range(16):
            transform(nat_bo, h * 2048, 2048, 0, f_bwo[h].rearrange("s p f -> p s f"), gates=cast_gate)
        state["pre_sd_list"] = [(f"sd{i}", w.at(f"sd{i}")) for i in range(4)]
        ftstate = [Slot(), Slot()]
        ftcnt = [0]

        def ftransform(natsrc, r0, c0, dst_ap):
            i = ftcnt[0] % 2
            ftcnt[0] += 1
            sem = f"tt{i}"
            buf = H(20512 + i * 2048, 20512 + i * 2048 + 2048)
            w.do("sync", lambda e, b=buf: e.dma_start(
                out=b, in_=natsrc[r0:r0 + 2048, c0:c0 + 128], transpose=True),
                waits=pre_gates + [("pl", state["quant_pl"])] + ftstate[i].f(),
                inc=(sem, 16))
            lv = w.at(sem)
            bsrc = buf.rearrange("p (e f) -> p e f", f=dst_ap.shape[2])
            ssem = f"sd{i}"
            w.do("scalar", lambda e, b=bsrc, d=dst_ap: e.dma_start(out=d, in_=b),
                 waits=[(sem, lv)], inc=(ssem, 16))
            ftstate[i].free = {ssem: w.at(ssem)}

        thunks = []
        for hg4 in range(8):
            for dt in range(16):
                thunks.append((ftransform, (nat_gu, hg4 * 2048, dt * 128,
                               f_gu[hg4 * 4:(hg4 + 1) * 4, dt].rearrange("e p f -> p e f"))))
        for ht in range(64):
            thunks.append((ftransform, (nat_pr, 0, ht * 128,
                           f_pr[ht].rearrange("s p f -> p s f"))))
        state["ft_thunks"] = thunks

        def drain_thunks(k):
            th = state["ft_thunks"]
            for _ in range(min(k, len(th))):
                fn, args = th.pop(0)
                fn(*args)
            state["ffn_sd_list"] = [(f"sd{i}", w.at(f"sd{i}")) for i in range(4)]
        state["drain_thunks"] = drain_thunks
        state["ffn_sd_list"] = state["pre_sd_list"]
        state["resid_gr"] = 0
        S_qt.free = {f"sd{i}": w.at(f"sd{i}") for i in range(4)}

        for sub in range(n_sub):
            kind = ("attn_a", "ffn", "attn_b", "ffn")[sub]
            for seq in range(1 if DEBUG_TRUNC else BPC):
                if kind == "attn_a":
                    emit_attn_a_seq(sub, seq)
                elif kind == "attn_b":
                    emit_attn_b_seq(sub, seq)
                else:
                    emit_ffn_seq(sub, seq)
            w.do("gpsimd", lambda e: e.nop(),
                 waits=[("go0", w.at("go0")), ("go1", w.at("go1"))], inc=("gr", 1))
            state["resid_gr"] = w.at("gr")

        if DEBUG_TRUNC:
            w.do("gpsimd", lambda e: e.dma_start(out=dbg[:, 0:8192], in_=aux3[:, :]),
                 waits=[("dv", w.at("dv")), ("ac", w.at("ac")), ("pe", w.at("pe"))],
                 inc=("gp", 16))
            w.do("gpsimd", lambda e: e.dma_start(
                out=dbg[:, 8192:16384].rearrange("p (a f) -> p a f", f=8192),
                in_=harena[:, 8192:16384].rearrange("p (a f) -> p a f", f=8192)),
                waits=[("gp", w.at("gp"))], inc=("gp", 16))
            w.do("gpsimd", lambda e: e.nop(), waits=[("gp", w.at("gp"))])
        w.do("gpsimd", lambda e: e.nop(),
             waits=[("go0", w.at("go0")), ("go1", w.at("go1")), ("gr", w.at("gr")),
                    ("gk", w.at("gk"))] + [(f"sd{i}", w.at(f"sd{i}")) for i in range(4)])

    def resid_src(sub):
        return xin if sub == 0 else out

    def emit_ln_chunk(sub, seq, c):
        base = seq * S + c * CH
        for tt in range(NTT):
            sl = tt % 2
            xts = xt[sl]
            sem = f"lx{sl}"
            w.do("sync", lambda e, x=xts, o=base + tt * 128, s=sub: e.dma_start(
                out=x[:, :], in_=resid_src(s)[o:o + 128, :]),
                waits=[("gr", state["resid_gr"])] + state["pre_sd_list"] + S_xt[sl].f(),
                inc=(sem, 16))
            lv = w.at(sem)

            w.do("vector", lambda e, x=xts: e.reduce_sum(small[:, 0:1], x[:, :], AX),
                 waits=[(sem, lv)])
            w.do("vector", lambda e: e.tensor_single_scalar(
                small[:, 1:2], small[:, 0:1], -1.0 / 2048.0, ALU.mult))
            w.do("scalar", lambda e, x=xts: e.activation(
                sqdump[:, :], x[:, :], AF.Square, bias=small[:, 1:2],
                accum_out=small[:, 2:3]),
                waits=[("dv", w.at("dv"))], inc=("ac", 1))
            ac_sq_ln = w.at("ac")
            w.do("vector", lambda e: e.tensor_scalar(small[:, 3:4], small[:, 2:3],
                                                     1.0 / 2048.0, EPS_LN, ALU.mult, ALU.add),
                 waits=[("ac", w.at("ac"))], inc=("dv", 1))

            w.do("scalar", lambda e: e.activation(small[:, 4:5], small[:, 3:4], AF.Ln),
                 waits=[("dv", w.at("dv"))])
            w.do("scalar", lambda e: e.activation(small[:, 5:6], small[:, 4:5], AF.Exp,
                                                  scale=-0.5))
            w.do("vector", lambda e, x=xts: e.tensor_scalar(
                xnb[:, :], x[:, :], small[:, 1:2], small[:, 5:6], ALU.add, ALU.mult),
                waits=[("ac", w.at("ac"))] + S_xnb.f(), inc=("dv", 1))
            dv_xn = w.at("dv")
            S_xnb.set_ready("dv", dv_xn)
            S_xt[sl].set_ready("ld", lv)
            S_xt[sl].consume("dv", dv_xn)
            S_xt[sl].consume("ac", ac_sq_ln)
            for bq in range(4):
                ps = bq % 2

                def tp(e, bq=bq, ps=ps):
                    last = None
                    for k in range(4):
                        dtile = bq * 4 + k
                        last = e.transpose(pt[ps][:, k * 128:(k + 1) * 128],
                                           xnb[:, dtile * 128:(dtile + 1) * 128],
                                           identb[:, :])
                    return last
                w.do("tensor", tp, waits=[("dv", dv_xn), ("ld", state["consts_ld"])] + S_pt[ps].f(),
                     inc=("pe", 1))
                pv = w.at("pe")
                S_pt[ps].set_ready("pe", pv)

                def cp(e, bq=bq, ps=ps, tt=tt):
                    o = xnT[:, :].rearrange("p (d q) -> p d q", q=512)
                    dst = bass.AP(o.tensor, o.offset + bq * 4 * 512 + tt * 128,
                                  [o.ap[0], [512, 4], [1, 128]])
                    src = pt[ps][:, 0:512].rearrange("p (d q) -> p d q", q=128)
                    return e.tensor_copy(dst, src)
                w.do("vector", cp, waits=[("pe", pv)] + S_xnT.f(), inc=("dv", 1))
                S_pt[ps].consume("dv", w.at("dv"))
            S_xnb.consume("pe", w.at("pe"))
        S_xnT.set_ready("dv", w.at("dv"))

    def rsqrt_small_emit(dst, srcvar, waits):
        w.do("scalar", lambda e: e.activation(small[:, 6:7], srcvar, AF.Ln), waits=waits)
        w.do("scalar", lambda e: e.activation(dst, small[:, 6:7], AF.Exp, scale=-0.5))

    def emit_ffn_seq(sub, seq):
        w.do("vector", lambda e: e.engine_nop(),
             waits=[("pl", state.get("quant_pl", 0))], inc=("dv", 1))
        for c in range(NCH):
            emit_ln_chunk(sub, seq, c)
            base = seq * S + c * CH
            pending_ones = [None]

            def emit_pending_ones():
                if pending_ones[0] is None:
                    return
                dv_h2, th, g = pending_ones[0]
                pending_ones[0] = None
                w.do("tensor", lambda e, t=th, g=g: e.matmul(
                    p1024[0:1, 0:512], onesb[:, 0:1], tmpb[t][:, :],
                    start=(g == 0), stop=(g == 63)),
                    waits=[("dv", dv_h2), ("ld", state["consts_ld"])]
                    + (S_p10[0].f() if g == 0 else []),
                    inc=("pe", 1))
                S_tmpb[th].consume("pe", w.at("pe"))
            for j in range(16):
                for half, (slot_sb, slot_st, hgsrc) in enumerate(
                        ((wsA, S_ws[0], j), (wsB, S_ws[1], 16 + j))):
                    sem = "lA" if half == 0 else "lB"
                    w.do("sync", lambda e, sbuf=slot_sb, hg=hgsrc: e.dma_start(
                        out=sbuf[:, :].rearrange("p (d f) -> p d f", f=512), in_=f_gu[hg].rearrange("d p f -> p d f")),
                        waits=state["ffn_sd_list"] + slot_st.f(), inc=(sem, 16))
                    slot_st.set_ready(sem, w.at(sem))
                for u in range(4):
                    g = 4 * j + u
                    pg = p512[0] if g % 2 == 0 else p512[2]
                    pu = p512[1] if g % 2 == 0 else p512[3]
                    ig = 0 if g % 2 == 0 else 2
                    iu = 1 if g % 2 == 0 else 3
                    emit_pending_ones()

                    def mmg(e, u=u, p=pg):
                        last = None
                        for dt in range(16):
                            last = e.matmul(p[:, :], wsA[:, dt * 512 + u * 128: dt * 512 + u * 128 + 128],
                                            xnT[:, dt * 512:(dt + 1) * 512],
                                            start=(dt == 0), stop=(dt == 15))
                        return last
                    w.do("tensor", mmg, waits=S_ws[0].r() + S_xnT.r() + S_p[ig].f(),
                         inc=("pe", 1))
                    pe_g = w.at("pe")

                    def mmu(e, u=u, p=pu):
                        last = None
                        for dt in range(16):
                            last = e.matmul(p[:, :], wsB[:, dt * 512 + u * 128: dt * 512 + u * 128 + 128],
                                            xnT[:, dt * 512:(dt + 1) * 512],
                                            start=(dt == 0), stop=(dt == 15))
                        return last
                    w.do("tensor", mmu, waits=S_ws[1].r() + S_xnT.r() + S_p[iu].f(),
                         inc=("pe", 1))
                    pe_u = w.at("pe")
                    if u == 3:
                        S_ws[0].consume("pe", pe_g)
                        S_ws[1].consume("pe", pe_u)
                    ts = g % 2
                    w.do("scalar", lambda e, p=pg, t=ts: e.activation(
                        tmpb[t][:, :], p[:, :], AF.Silu),
                        waits=[("pe", pe_g)] + S_tmpb[ts].f(), inc=("ac", 1))
                    ac_s = w.at("ac")
                    S_p[ig].consume("ac", ac_s)
                    w.do("vector", lambda e, p=pu, t=ts, g=g: e.tensor_tensor(
                        H(g * 512, g * 512 + 512), tmpb[t][:, :], p[:, :], ALU.mult),
                        waits=[("ac", ac_s), ("pe", pe_u)], inc=("dv", 1))
                    dv_h = w.at("dv")
                    S_p[iu].consume("dv", dv_h)
                    S_tmpb[ts].consume("dv", dv_h)
                    th = 2 + g % 2
                    w.do("vector", lambda e, g=g, t=th: e.tensor_tensor(
                        tmpb[t][:, :], H(g * 512, g * 512 + 512), H(g * 512, g * 512 + 512),
                        ALU.mult),
                        waits=S_tmpb[th].f(), inc=("dv", 1))
                    dv_h2 = w.at("dv")
                    pending_ones[0] = (dv_h2, th, g)
            emit_pending_ones()
            pe_sums = w.at("pe")
            S_xnT.free = {"pe": pe_sums}
            S_har.set_ready("dv", w.at("dv"))
            w.do("vector", lambda e: e.tensor_copy(qf32[0:1, 0:512], p1024[0:1, 0:512]),
                 waits=[("pe", pe_sums)] + S_tmpf[0].f(), inc=("dv", 1))
            dv_sums = w.at("dv")
            S_p10[0].consume("dv", dv_sums)
            for tti in range(NTT):
                w.do("tensor", lambda e, t=tti: e.transpose(
                    p1024[:, 640:768], qf32[:, t * 128:(t + 1) * 128], identf[:, :]),
                    waits=[("dv", dv_sums), ("ld", state["consts_ld"])] + S_p10[1].f(), inc=("pe", 1))
                pe_t = w.at("pe")
                w.do("vector", lambda e: e.tensor_copy(small[:, 7:8], p1024[:, 640:641]),
                     waits=[("pe", pe_t), ("ac", w.at("ac"))])
                w.do("vector", lambda e: e.tensor_scalar(
                    small[:, 8:9], small[:, 7:8], 1.0 / 8192.0, EPS_RMS,
                    ALU.mult, ALU.add))
                S_p10[1].consume("dv", w.at("dv"))
                rsqrt_small_emit(invrms[:, tti:tti + 1], small[:, 8:9],
                                 [("dv", w.at("dv"))])
            ac_inv = w.at("ac")
            S_tmpf[0].consume("pe", w.at("pe"))
            for ds in range(4):
                for htg in range(4):
                    sl = htg % 2
                    slot_sb = wsA if sl == 0 else wsB
                    sem = "lA" if sl == 0 else "lB"
                    w.do("sync", lambda e, sbuf=slot_sb, htg=htg, ds=ds: e.dma_start(
                        out=sbuf[:, :].rearrange("p (t f) -> p t f", f=512),
                        in_=f_pr[htg * 16:(htg + 1) * 16, ds].rearrange("t p f -> p t f")),
                        waits=state["ffn_sd_list"] + S_ws[sl].f(), inc=(sem, 16))
                    S_ws[sl].set_ready(sem, w.at(sem))
                    for tt in range(NTT):
                        def mmp(e, htg=htg, tt=tt, sbuf=slot_sb):
                            last = None
                            for hh in range(16):
                                ht = htg * 16 + hh
                                last = e.matmul(p512[tt][:, :],
                                                H(ht * 512 + tt * 128, ht * 512 + tt * 128 + 128),
                                                sbuf[:, hh * 512:(hh + 1) * 512],
                                                start=(ht == 0), stop=(ht == 63))
                            return last
                        w.do("tensor", mmp, waits=S_ws[sl].r() + S_har.r()
                             + (S_p[tt].f() if htg == 0 else []), inc=("pe", 1))
                    S_ws[sl].consume("pe", w.at("pe"))
                pe_proj = w.at("pe")
                for tt in range(NTT):
                    sl = tt % 2
                    sem = f"lr{sl}"
                    w.do("sync", lambda e, tt=tt, ds=ds, s=sub: e.dma_start(
                        out=xrd[sl][:, :],
                        in_=resid_src(s)[base + tt * 128: base + tt * 128 + 128,
                                         ds * 512:(ds + 1) * 512]),
                        waits=S_xrd[sl].f(), inc=(sem, 16))
                    lv = (sem, w.at(sem))
                    osl = (ds * 4 + tt) % 2
                    w.do("scalar", lambda e, tt=tt, osl=osl: e.activation(
                        sqdump[:, osl * 512: osl * 512 + 512], p512[tt][:, :],
                        AF.Copy, scale=invrms[:, tt:tt + 1]),
                        waits=[("pe", pe_proj), ("ac", ac_inv),
                               ("dv", state.get("ffn_evac_dv", 0))], inc=("ac", 1))
                    ac_o = w.at("ac")
                    w.do("vector", lambda e, sl=sl, osl=osl: e.tensor_tensor(
                        outsb[osl][:, :], sqdump[:, osl * 512: osl * 512 + 512],
                        xrd[sl][:, :], ALU.add),
                        waits=[("ac", ac_o), lv] + S_out[osl].f(), inc=("dv", 1))
                    dv_o = w.at("dv")
                    state["ffn_evac_dv"] = dv_o
                    S_p[tt].consume("ac", ac_o)
                    S_xrd[sl].consume("dv", dv_o)
                    w.do("gpsimd", lambda e, tt=tt, ds=ds, osl=osl: e.dma_start(
                        out=out[base + tt * 128: base + tt * 128 + 128,
                                ds * 512:(ds + 1) * 512],
                        in_=outsb[osl][:, :]),
                        waits=[("dv", dv_o)], inc=(f"go{osl}", 16))
                    S_out[osl].consume(f"go{osl}", w.at(f"go{osl}"))
            S_har.free = {"pe": w.at("pe")}

    def dump(cond, srcap, col0, n, wait_sems, psum=False):
        if not (DEBUG_A == 1 and cond):
            return
        prior = state.get("dump_gp", 0)
        if psum:
            w.do("vector", lambda e: e.tensor_copy(sqdump[:, 0:n], srcap),
                 waits=wait_sems + [("gp", prior)], inc=("dv", 1))
            dvv = w.at("dv")
            w.do("gpsimd", lambda e: e.dma_start(out=dbg[:, col0:col0 + n],
                                                 in_=sqdump[:, 0:n]),
                 waits=[("dv", dvv), ("gp", prior)], inc=("gp", 16))
        else:
            w.do("gpsimd", lambda e: e.dma_start(out=dbg[:, col0:col0 + n], in_=srcap),
                 waits=wait_sems + [("gp", prior)], inc=("gp", 16))
        state["dump_gp"] = w.at("gp")
        w.do("vector", lambda e: e.engine_nop(), waits=[("gp", w.at("gp"))])
        w.do("scalar", lambda e: e.nop(), waits=[("gp", w.at("gp"))], inc=("ac", 1))

    def emit_attn_a_seq(sub, seq):
        KC = lambda kv, et: (kv * 2 + et) * 2048
        VC = lambda kv, blk: 8192 + kv * (16 * 257) + blk * 257
        QB = (16416, 18464)
        KVB = (18464, 18976)
        PA0, PA1 = 18976, 19616
        OB = (20256, 20512)
        nch = DEBUG_TRUNC if DEBUG_TRUNC else NCH

        def vca_ones(e):
            o = harena[:, :]
            ap = bass.AP(o.tensor, o.offset + 8192 + 256,
                         [o.ap[0], [16 * 257, 2], [257, 16], [1, 1]])
            return e.memset(ap, 1.0)
        w.do("vector", vca_ones, waits=S_har.f() + state["pre_gates"]
             + state["pre_sd_list"], inc=("dv", 1))
        S_har.reset()

        w.do("sync", lambda e: e.dma_start(
            out=wsB[:, :].rearrange("p (d f) -> p d f", f=512),
            in_=f_awv[0].rearrange("d p f -> p d f")),
            waits=state["pre_sd_list"] + S_ws[1].f(), inc=("lB", 16))
        S_ws[1].set_ready("lB", w.at("lB"))
        for c in range(nch):
            emit_ln_chunk(sub, seq, c)
            for tt in range(NTT):
                def mmv(e, tt=tt):
                    last = None
                    for dt in range(16):
                        last = e.matmul(p512[3][:, :],
                                        xnT[:, dt * 512 + tt * 128: dt * 512 + tt * 128 + 128],
                                        wsB[:, dt * 512:(dt + 1) * 512],
                                        start=(dt == 0), stop=(dt == 15))
                    return last
                w.do("tensor", mmv, waits=S_ws[1].r() + S_xnT.r() + S_p[3].f(),
                     inc=("pe", 1))
                pe_v = w.at("pe")
                w.do("vector", lambda e: e.tensor_copy(qf32[:, 512:1024], p512[3][:, :]),
                     waits=[("pe", pe_v)], inc=("dv", 1))
                S_p[3].consume("dv", w.at("dv"))
                emit_headln_only(tt, qf32[:, 512:1024], 2, dst=H(*KVB),
                                 extra_wait=[("dv", state.get("kvb_dv", 0))])

                def cpv(e, tt=tt, c=c):
                    o = harena[:, :]
                    blk = c * 4 + tt
                    dst = bass.AP(o.tensor, o.offset + 8192 + blk * 257,
                                  [o.ap[0], [16 * 257, 2], [1, 256]])
                    src = H(*KVB).rearrange("p (kv f) -> p kv f", f=256)
                    return e.tensor_copy(dst, src)
                w.do("vector", cpv, inc=("dv", 1))
                state["kvb_dv"] = w.at("dv")
            S_xnT.free = {"pe": w.at("pe")}
            if seq == 1:
                state["drain_thunks"](24)
        dv_vall = w.at("dv")
        for kv in range(2):
            def msall(e, kv=kv):
                last = None
                for kt in range(16):
                    last = e.matmul(p1024[0:1, 704:961], onesb[:, 0:1],
                                    harena[:, VC(kv, kt): VC(kv, kt) + 257],
                                    start=(kt == 0), stop=(kt == 15))
                return last
            w.do("tensor", msall, waits=[("dv", dv_vall), ("ld", state["consts_ld"]),
                                         ("dv", state.get("scov_dv", 0))]
                 + S_p10[1].f(), inc=("pe", 1))
            pe_s = w.at("pe")
            w.do("vector", lambda e, kv=kv: e.tensor_copy(
                sall[0:1, kv * 257:(kv + 1) * 257], p1024[0:1, 704:961]),
                waits=[("pe", pe_s)], inc=("dv", 1))
            state["scov_dv"] = w.at("dv")
        S_p10[1].consume("dv", w.at("dv"))

        for c in range(nch):
            emit_ln_chunk(sub, seq, c)
            base = seq * S + c * CH
            for tt in range(NTT):
                t0 = c * CH + tt * 128
                w.do("sync", lambda e, t0=t0: e.dma_start(
                    out=ropeC[:, :], in_=c_cos[t0:t0 + 128, :]),
                    waits=[("dv", w.at("dv"))], inc=("lt", 16))
                w.do("sync", lambda e, t0=t0: e.dma_start(
                    out=ropeS[:, :], in_=c_sin[t0:t0 + 128, :]),
                    inc=("lt", 16))
                rope_ld = ("lt", w.at("lt"))
                for es in range(4):
                    sl = es % 2
                    slot_sb = wsA if sl == 0 else wsB
                    sem = "lA" if sl == 0 else "lB"
                    w.do("sync", lambda e, sbuf=slot_sb, es=es: e.dma_start(
                        out=sbuf[:, :].rearrange("p (d f) -> p d f", f=512),
                        in_=f_awq[es].rearrange("d p f -> p d f")),
                        waits=state["pre_sd_list"] + S_ws[sl].f(), inc=(sem, 16))
                    S_ws[sl].set_ready(sem, w.at(sem))
                    pp = es % 2

                    def mmq(e, es=es, tt=tt, sbuf=slot_sb, pp=pp):
                        last = None
                        for dt in range(16):
                            last = e.matmul(p512[pp][:, :],
                                            xnT[:, dt * 512 + tt * 128: dt * 512 + tt * 128 + 128],
                                            sbuf[:, dt * 512:(dt + 1) * 512],
                                            start=(dt == 0), stop=(dt == 15))
                        return last
                    w.do("tensor", mmq, waits=S_ws[sl].r() + S_xnT.r() + S_p[pp].f(),
                         inc=("pe", 1))
                    pe_q = w.at("pe")
                    S_ws[sl].consume("pe", pe_q)
                    w.do("vector", lambda e, es=es, pp=pp: e.tensor_copy(
                        qf32[:, es * 512:(es + 1) * 512], p512[pp][:, :]),
                        waits=[("pe", pe_q)], inc=("dv", 1))
                    S_p[pp].consume("dv", w.at("dv"))
                emit_rope_ln(tt, qf32[:, :], 8, rope_ld, scale16=True, dst=H(*QB),
                             extra_wait=[("pe", state.get("qb_pe", 0))])
                dv_qb = w.at("dv")
                emit_transpose_tiles(H(*QB), 16, dv_qb,
                                     lambda dtile: (dtile * 512 + tt * 128), S_qt, aux2)
                state["qb_pe"] = w.at("pe")
                sl = 0
                w.do("sync", lambda e: e.dma_start(
                    out=wsA[:, :].rearrange("p (d f) -> p d f", f=512),
                    in_=f_awk[0].rearrange("d p f -> p d f")),
                    waits=state["pre_sd_list"] + S_ws[0].f(), inc=("lA", 16))
                S_ws[0].set_ready("lA", w.at("lA"))

                def mmk(e, tt=tt):
                    last = None
                    for dt in range(16):
                        last = e.matmul(p512[2][:, :],
                                        xnT[:, dt * 512 + tt * 128: dt * 512 + tt * 128 + 128],
                                        wsA[:, dt * 512:(dt + 1) * 512],
                                        start=(dt == 0), stop=(dt == 15))
                    return last
                w.do("tensor", mmk, waits=S_ws[0].r() + S_xnT.r() + S_p[2].f(),
                     inc=("pe", 1))
                pe_k = w.at("pe")
                S_ws[0].consume("pe", pe_k)
                w.do("vector", lambda e: e.tensor_copy(qf32[:, 0:512], p512[2][:, :]),
                     waits=[("pe", pe_k)], inc=("dv", 1))
                S_p[2].consume("dv", w.at("dv"))
                emit_rope_ln(tt, qf32[:, 0:512], 2, rope_ld, scale16=False,
                             dst=H(*KVB), extra_wait=[("pe", state.get("kvb_pe", 0))])
                dv_kb = w.at("dv")
                ps = tt % 2

                def tpk(e, ps=ps):
                    last = None
                    for k in range(4):
                        last = e.transpose(pt[ps][:, k * 128:(k + 1) * 128],
                                           H(*KVB)[:, k * 128:(k + 1) * 128],
                                           identb[:, :])
                    return last
                w.do("tensor", tpk, waits=[("dv", dv_kb)] + S_pt[ps].f(), inc=("pe", 1))
                pv = w.at("pe")
                S_pt[ps].set_ready("pe", pv)
                state["kvb_pe"] = pv

                def cpk(e, ps=ps, tt=tt, c=c):
                    o = harena[:, :]
                    dst = bass.AP(o.tensor, o.offset + c * 512 + tt * 128,
                                  [o.ap[0], [2048, 4], [1, 128]])
                    src = pt[ps][:, 0:512].rearrange("p (d q) -> p d q", q=128)
                    return e.tensor_copy(dst, src)
                w.do("vector", cpk, waits=[("pe", pv)], inc=("dv", 1))
                S_pt[ps].consume("dv", w.at("dv"))
            S_xnT.free = {"pe": w.at("pe")}
            dv_kv = w.at("dv")
            pa_pe = state.setdefault("pa_pe", {0: 0, 1: 0})
            for qt in range(NTT):
                Qg = c * 4 + qt
                kts = list(range(max(0, Qg - 4), Qg + 1))
                nk = len(kts)
                for kv in range(2):
                    def mcov(e, kts=kts, kv=kv):
                        last = None
                        for i, kt in enumerate(kts):
                            last = e.matmul(p1024[0:1, 704:961], onesb[:, 0:1],
                                            harena[:, VC(kv, kt): VC(kv, kt) + 257],
                                            start=(i == 0), stop=(i == len(kts) - 1))
                        return last
                    w.do("tensor", mcov, waits=[("dv", state.get("scov_dv", 0))]
                         + [("dv", dv_kv)], inc=("pe", 1))
                    pe_c = w.at("pe")
                    w.do("vector", lambda e, kv=kv: e.tensor_sub(
                        srowb[0:1, kv * 257:(kv + 1) * 257],
                        sall[0:1, kv * 257:(kv + 1) * 257], p1024[0:1, 704:961]),
                        waits=[("pe", pe_c)], inc=("dv", 1))
                    state["scov_dv"] = w.at("dv")
                dv_srow = w.at("dv")
                for h in range(HA):
                    kv = h // 4
                    par = (qt * 8 + h) % 2
                    if par == 0:
                        def mks(e, kts=kts, h=h, qt=qt, kv=kv):
                            last = None
                            for i, kt in enumerate(kts):
                                for et in range(2):
                                    last = e.matmul(
                                        p1024[:, i * 128:(i + 1) * 128],
                                        harena[:, KC(kv, et) + kt * 128: KC(kv, et) + kt * 128 + 128],
                                        aux2[:, (h * 2 + et) * 512 + qt * 128:
                                             (h * 2 + et) * 512 + qt * 128 + 128],
                                        start=(et == 0), stop=(et == 1))
                            return last
                        wl = S_p10[0].f() + S_p10[1].f()
                        w.do("tensor", mks, waits=S_qt.r() + wl
                             + [("dv", dv_kv), ("dv", dv_srow)], inc=("pe", 1))
                        pe_s = w.at("pe")
                        w.do("scalar", lambda e, nk=nk: e.activation(
                            p1024[:, 0:nk * 128], p1024[:, 0:nk * 128],
                            AF.Tanh, scale=1.0 / 30.0),
                            waits=[("pe", pe_s), ("pe", pa_pe[0])])
                        w.do("scalar", lambda e, nk=nk: e.activation(
                            H(PA0, PA0 + nk * 128), p1024[:, 0:nk * 128],
                            AF.Exp, scale=30.0))
                        ac_e = w.at("ac")
                        S_p10[0].consume("ac", ac_e)
                        S_p10[1].consume("ac", ac_e)
                        pa = PA0
                    else:
                        def mks(e, kts=kts, h=h, qt=qt, kv=kv):
                            last = None
                            for i, kt in enumerate(kts):
                                tgt = (p512[0][:, (i % 4) * 128:(i % 4) * 128 + 128] if i < 4
                                       else p512[1][:, 0:128])
                                for et in range(2):
                                    last = e.matmul(
                                        tgt,
                                        harena[:, KC(kv, et) + kt * 128: KC(kv, et) + kt * 128 + 128],
                                        aux2[:, (h * 2 + et) * 512 + qt * 128:
                                             (h * 2 + et) * 512 + qt * 128 + 128],
                                        start=(et == 0), stop=(et == 1))
                            return last
                        w.do("tensor", mks, waits=S_qt.r() + S_p[0].f() + S_p[1].f()
                             + [("dv", dv_kv), ("dv", dv_srow)], inc=("pe", 1))
                        pe_s = w.at("pe")
                        n0 = min(nk, 4)
                        w.do("scalar", lambda e, n0=n0: e.activation(
                            p512[0][:, 0:n0 * 128], p512[0][:, 0:n0 * 128],
                            AF.Tanh, scale=1.0 / 30.0),
                            waits=[("pe", pe_s), ("pe", pa_pe[1])])
                        w.do("scalar", lambda e, n0=n0: e.activation(
                            H(PA1, PA1 + n0 * 128), p512[0][:, 0:n0 * 128],
                            AF.Exp, scale=30.0))
                        if nk == 5:
                            w.do("scalar", lambda e: e.activation(
                                p512[1][:, 0:128], p512[1][:, 0:128],
                                AF.Tanh, scale=1.0 / 30.0))
                            w.do("scalar", lambda e: e.activation(
                                H(PA1 + 512, PA1 + 640), p512[1][:, 0:128],
                                AF.Exp, scale=30.0))
                        ac_e = w.at("ac")
                        S_p[0].consume("ac", ac_e)
                        S_p[1].consume("ac", ac_e)
                        pa = PA1
                    masked = {}
                    def mask_one(i, keepm, e30m, t, pa=pa, ac_e=ac_e):
                        w.do("vector", lambda e, i=i, keepm=keepm, t=t: e.tensor_tensor(
                            tmpb[t][:, 0:128], H(pa + i * 128, pa + i * 128 + 128),
                            keepm[:, :], ALU.mult),
                            waits=[("ac", ac_e), ("ld", state["consts_ld"])]
                            + S_tmpb[t].f())
                        w.do("vector", lambda e, e30m=e30m, t=t: e.tensor_tensor(
                            tmpb[t][:, 0:128], tmpb[t][:, 0:128], e30m[:, :], ALU.add))
                        S_tmpb[t].set_ready("dv", w.at("dv"))
                        masked[i] = t
                    mask_one(nk - 1, maskle, e30gt, 2)
                    if nk == 5:
                        mask_one(0, maskgt, e30le, 3)
                    dv_m = ("dv", w.at("dv"))
                    io = 2 + par

                    def mpv(e, kts=kts, pa=pa, kv=kv, io=io, masked=masked):
                        last = e.matmul(p512[io][:, 0:257], c_e30c_sb[0:1, :],
                                        srowb[0:1, kv * 257:(kv + 1) * 257],
                                        start=True, stop=False)
                        for i, kt in enumerate(kts):
                            lhs = (tmpb[masked[i]][:, 0:128] if i in masked
                                   else H(pa + i * 128, pa + i * 128 + 128))
                            last = e.matmul(p512[io][:, 0:257], lhs,
                                            harena[:, VC(kv, kt): VC(kv, kt) + 257],
                                            start=False, stop=(i == len(kts) - 1))
                        return last
                    w.do("tensor", mpv, waits=[("ac", ac_e), dv_m,
                                               ("dv", dv_srow), ("dv", dv_kv)]
                         + S_p[io].f(), inc=("pe", 1))
                    pe_pv = w.at("pe")
                    pa_pe[par] = pe_pv
                    for i, t in masked.items():
                        S_tmpb[t].consume("pe", pe_pv)
                    w.do("vector", lambda e, io=io: e.reciprocal(
                        small[:, 60:61], p512[io][:, 256:257]),
                        waits=[("pe", pe_pv), ("ac", state.get("norm_ac", 0))],
                        inc=("dv", 1))
                    dv_r = w.at("dv")
                    w.do("scalar", lambda e, io=io: e.activation(
                        H(*OB), p512[io][:, 0:256], AF.Copy, scale=small[:, 60:61]),
                        waits=[("dv", dv_r), ("pe", state.get("ob_pe", 0))],
                        inc=("ac", 1))
                    dv_n = w.at("ac")
                    state["norm_ac"] = dv_n
                    S_p[io].consume("ac", dv_n)
                    ps = (h + qt) % 2

                    def tpo(e, ps=ps):
                        last = None
                        for et in range(2):
                            last = e.transpose(pt[ps][:, et * 128:(et + 1) * 128],
                                               harena[:, OB[0] + et * 128: OB[0] + et * 128 + 128],
                                               identb[:, :])
                        return last
                    w.do("tensor", tpo, waits=[("ac", dv_n)] + S_pt[ps].f(), inc=("pe", 1))
                    pv = w.at("pe")
                    state["ob_pe"] = pv
                    S_pt[ps].set_ready("pe", pv)

                    def cpo(e, ps=ps, h=h, qt=qt):
                        o = aux3[:, :]
                        dst = bass.AP(o.tensor, o.offset + (h * 2) * 512 + qt * 128,
                                      [o.ap[0], [512, 2], [1, 128]])
                        src = pt[ps][:, 0:256].rearrange("p (d q) -> p d q", q=128)
                        return e.tensor_copy(dst, src)
                    w.do("vector", cpo, waits=[("pe", pv)] + S_ot.f(), inc=("dv", 1))
                    S_pt[ps].consume("dv", w.at("dv"))
            S_ot.set_ready("dv", w.at("dv"))
            emit_outproj(sub, base, f_awo, 16, S_ot, aux3)
            S_ot.free = {"pe": w.at("pe")}
            S_qt.free = {"pe": w.at("pe")}
            if seq == 1:
                state["drain_thunks"](24)
        if seq == 1:
            state["drain_thunks"](10 ** 9)
        S_har.free = {"pe": w.at("pe")}

    def emit_rope_ln(tt, srcf32, nheads, rope_ld, scale16, dst, extra_wait=()):
        nh = nheads

        cv = bass.AP(ropeC, 0, [ropeC[:, :].ap[0], [0, nh], [1, 128]])
        sv = bass.AP(ropeS, 0, [ropeS[:, :].ap[0], [0, nh], [1, 128]])
        q1 = srcf32.rearrange("p (h s f) -> p h s f", s=2, f=128)
        q1a = bass.AP(q1.tensor, q1.offset, [q1.ap[0], q1.ap[1], q1.ap[3]])
        q2a = bass.AP(q1.tensor, q1.offset + 128, [q1.ap[0], q1.ap[1], q1.ap[3]])
        t1a = tmpf1[:, 0:nh * 128].rearrange("p (h f) -> p h f", f=128)
        t1b = tmpf1[:, 1024:1024 + nh * 128].rearrange("p (h f) -> p h f", f=128)
        sd = sqdump[:, 0:nh * 256].rearrange("p (h s f) -> p h s f", s=2, f=128)
        o1 = bass.AP(sd.tensor, sd.offset, [sd.ap[0], sd.ap[1], sd.ap[3]])
        o2 = bass.AP(sd.tensor, sd.offset + 128, [sd.ap[0], sd.ap[1], sd.ap[3]])
        w.do("vector", lambda e: e.tensor_tensor(t1a, q2a, sv, ALU.mult),
             waits=[rope_ld] + S_tmpf[1].f())
        w.do("vector", lambda e: e.tensor_tensor(t1b, q1a, cv, ALU.mult))
        w.do("vector", lambda e: e.tensor_tensor(o1, t1b, t1a, ALU.subtract))
        w.do("vector", lambda e: e.tensor_tensor(t1a, q2a, cv, ALU.mult))
        w.do("vector", lambda e: e.tensor_tensor(t1b, q1a, sv, ALU.mult))
        w.do("vector", lambda e: e.tensor_tensor(o2, t1a, t1b, ALU.add))
        S_tmpf[1].consume("dv", w.at("dv"))
        emit_headln_only(tt, sqdump[:, 0:nh * 256], nheads, dst, scale16=scale16,
                         extra_wait=extra_wait)

    def emit_headln_only(tt, srcf32, nheads, dst, scale16=False, extra_wait=()):
        nh = nheads

        svh = srcf32.rearrange("p (h f) -> p h f", f=256)
        w.do("vector", lambda e: e.reduce_sum(small[:, 8:8 + nh], svh, AX))
        w.do("vector", lambda e: e.tensor_single_scalar(
            small[:, 16:16 + nh], small[:, 8:8 + nh], -1.0 / 256.0, ALU.mult))

        def center(e):
            sv = srcf32.rearrange("p (h f) -> p h f", f=256)
            mb = small[:, 16:16 + nh][:, :, None].broadcast_to([128, nh, 256])
            return e.tensor_tensor(qf32[:, 0:nh * 256].rearrange("p (h f) -> p h f", f=256),
                                   sv, mb, ALU.add)
        w.do("vector", center, waits=S_tmpf[0].f(), inc=("dv", 1))
        dv_c = w.at("dv")
        w.do("scalar", lambda e: e.activation(sqdump[:, 0:nh * 256],
                                              qf32[:, 0:nh * 256], AF.Square),
             waits=[("dv", dv_c)], inc=("ac", 1))
        ac_sq = w.at("ac")

        sqv = sqdump[:, 0:nh * 256].rearrange("p (h f) -> p h f", f=256)
        w.do("vector", lambda e: e.reduce_sum(small[:, 24:24 + nh], sqv, AX),
             waits=[("ac", ac_sq)])
        w.do("vector", lambda e: e.tensor_scalar(
            small[:, 32:32 + nh], small[:, 24:24 + nh], 1.0 / 256.0, EPS_HEAD,
            ALU.mult, ALU.add))

        w.do("scalar", lambda e: e.activation(small[:, 40:40 + nh],
                                              small[:, 32:32 + nh], AF.Ln),
             waits=[("dv", w.at("dv"))])
        w.do("scalar", lambda e: e.activation(small[:, 48:48 + nh],
                                              small[:, 40:40 + nh], AF.Exp, scale=-0.5))
        ac_i = w.at("ac")
        if scale16:
            w.do("vector", lambda e: e.tensor_single_scalar(
                small[:, 48:48 + nh], small[:, 48:48 + nh], 16.0, ALU.mult),
                waits=[("ac", ac_i)], inc=("dv", 1))

        def apply(e):
            iv = small[:, 48:48 + nh][:, :, None].broadcast_to([128, nh, 256])
            return e.tensor_tensor(dst.rearrange("p (h f) -> p h f", f=256),
                                   qf32[:, 0:nh * 256].rearrange("p (h f) -> p h f", f=256),
                                   iv, ALU.mult)
        w.do("vector", apply, waits=[("ac", ac_i)] + list(extra_wait), inc=("dv", 1))
        S_tmpf[0].consume("dv", w.at("dv"))

    def emit_transpose_tiles(srcb, ntiles, dv_ready, dstcol, slot, dsttensor):
        for bq in range(ntiles // 4):
            ps = bq % 2

            def tp(e, bq=bq, ps=ps):
                last = None
                for k in range(4):
                    i = bq * 4 + k
                    last = e.transpose(pt[ps][:, k * 128:(k + 1) * 128],
                                       srcb[:, i * 128:(i + 1) * 128], identb[:, :])
                return last
            w.do("tensor", tp, waits=[("dv", dv_ready), ("ld", state["consts_ld"])] + S_pt[ps].f(),
                 inc=("pe", 1))
            pv = w.at("pe")
            S_pt[ps].set_ready("pe", pv)

            def cp(e, bq=bq, ps=ps):
                o = dsttensor[:, :]
                dst = bass.AP(o.tensor, o.offset + dstcol(bq * 4),
                              [o.ap[0], [512, 4], [1, 128]])
                src = pt[ps][:, 0:512].rearrange("p (d q) -> p d q", q=128)
                return e.tensor_copy(dst, src)
            w.do("vector", cp, waits=[("pe", pv)] + slot.f(), inc=("dv", 1))
            S_pt[ps].consume("dv", w.at("dv"))
        slot.set_ready("dv", w.at("dv"))

    def emit_outproj(sub, base, fsrc, nacc, s_ot, ottensor):
        for ds in range(4):
            sl = ds % 2
            slot_sb = wsA if sl == 0 else wsB
            sem = "lA" if sl == 0 else "lB"
            w.do("sync", lambda e, sbuf=slot_sb, ds=ds, f=fsrc: e.dma_start(
                out=sbuf[:, :].rearrange("p (b f) -> p b f", f=512), in_=f[:, ds].rearrange("b p f -> p b f")),
                waits=state["pre_sd_list"] + S_ws[sl].f(), inc=(sem, 16))
            S_ws[sl].set_ready(sem, w.at(sem))
            for qt in range(NTT):
                pp = qt % 2

                def mmo(e, qt=qt, sbuf=slot_sb, pp=pp, nacc=nacc):
                    last = None
                    for bb in range(nacc):
                        last = e.matmul(p512[pp][:, :],
                                        ottensor[:, bb * 512 + qt * 128: bb * 512 + qt * 128 + 128],
                                        sbuf[:, bb * 512:(bb + 1) * 512],
                                        start=(bb == 0), stop=(bb == nacc - 1))
                    return last
                w.do("tensor", mmo, waits=S_ws[sl].r() + s_ot.r() + S_p[pp].f(),
                     inc=("pe", 1))
                pe_o = w.at("pe")
                if qt == NTT - 1:
                    S_ws[sl].consume("pe", pe_o)
                xsl = qt % 2
                sem2 = f"lr{xsl}"
                w.do("sync", lambda e, qt=qt, ds=ds, s=sub: e.dma_start(
                    out=xrd[xsl][:, :],
                    in_=resid_src(s)[base + qt * 128: base + qt * 128 + 128,
                                     ds * 512:(ds + 1) * 512]),
                    waits=S_xrd[xsl].f(), inc=(sem2, 16))
                lv = (sem2, w.at(sem2))
                dbgc = (DEBUG_A == 3 and base == 512 and fsrc is f_awo and ds == 0)
                dump(dbgc and qt == 0, p512[pp][:, :], 0, 512, [("pe", pe_o)], psum=True)
                dump(dbgc and qt == 0, xrd[xsl][:, :], 512, 512, [("ld", lv)])
                dump(dbgc and qt == 2, p512[pp][:, :], 2048, 512, [("pe", pe_o)], psum=True)
                osl = (ds * 4 + qt) % 2
                w.do("vector", lambda e, pp=pp, xsl=xsl, osl=osl: e.tensor_tensor(
                    outsb[osl][:, :], p512[pp][:, :], xrd[xsl][:, :], ALU.add),
                    waits=[("pe", pe_o), lv] + S_out[osl].f(), inc=("dv", 1))
                dv_o = w.at("dv")
                dump(dbgc and qt == 0, outsb[osl][:, :], 1024, 512, [("dv", dv_o)])
                S_p[pp].consume("dv", dv_o)
                S_xrd[xsl].consume("dv", dv_o)
                w.do("gpsimd", lambda e, qt=qt, ds=ds, osl=osl: e.dma_start(
                    out=out[base + qt * 128: base + qt * 128 + 128, ds * 512:(ds + 1) * 512],
                    in_=outsb[osl][:, :]),
                    waits=[("dv", dv_o)], inc=(f"go{osl}", 16))
                S_out[osl].consume(f"go{osl}", w.at(f"go{osl}"))

    def emit_attn_b_seq(sub, seq):
        KCUR = 0
        VCUR = 8192
        KOLD = [16448, 16448 + 1536]
        VOLD = [19520, 19520 + 1548]
        PT0 = 22616
        OBB = (30976, 31232)

        state.setdefault("ks_gp", {})
        for c in range(NCH):
            emit_ln_chunk(sub, seq, c)
            base = seq * S + c * CH
            def ones_b(e):
                o = harena[:, :]
                ap1 = bass.AP(o.tensor, o.offset + VCUR + 128,
                              [o.ap[0], [516, 16], [129, 4], [1, 1]])
                e.memset(ap1, 1.0)
                ap2 = bass.AP(o.tensor, o.offset + VOLD[0] + 128,
                              [o.ap[0], [129, 12], [1, 1]])
                e.memset(ap2, 1.0)
                ap3 = bass.AP(o.tensor, o.offset + VOLD[1] + 128,
                              [o.ap[0], [129, 12], [1, 1]])
                return e.memset(ap3, 1.0)
            w.do("vector", ones_b, waits=S_har.f() + S_vold[0].f() + S_vold[1].f()
                 + [("gkc", w.at("gkc")), ("gr", state["resid_gr"])], inc=("dv", 1))
            S_har.reset()
            vstg_pe = [0]
            for es in range(4):
                for wi, (fsrc, which) in enumerate(((f_bwq, "q"), (f_bwk, "k"), (f_bwv, "v"))):
                    sl = (es * 3 + wi) % 2
                    slot_sb = wsA if sl == 0 else wsB
                    sem = "lA" if sl == 0 else "lB"
                    w.do("sync", lambda e, sbuf=slot_sb, es=es, f=fsrc: e.dma_start(
                        out=sbuf[:, :].rearrange("p (d f) -> p d f", f=512), in_=f[es].rearrange("d p f -> p d f")),
                        waits=state["pre_sd_list"] + S_ws[sl].f(), inc=(sem, 16))
                    S_ws[sl].set_ready(sem, w.at(sem))
                    for hh in range(4):
                        h = es * 4 + hh
                        pp = hh % 2

                        def mmp(e, hh=hh, sbuf=slot_sb, pp=pp):
                            last = None
                            for dt in range(16):
                                last = e.matmul(p512[pp][:, :],
                                                sbuf[:, dt * 512 + hh * 128: dt * 512 + hh * 128 + 128],
                                                xnT[:, dt * 512:(dt + 1) * 512],
                                                start=(dt == 0), stop=(dt == 15))
                            return last
                        w.do("tensor", mmp, waits=S_ws[sl].r() + S_xnT.r() + S_p[pp].f(),
                             inc=("pe", 1))
                        pe_p = w.at("pe")
                        if hh == 3:
                            S_ws[sl].consume("pe", pe_p)
                        if which == "q":
                            w.do("vector", lambda e, h=h, pp=pp: e.tensor_copy(
                                aux2[:, h * 512:(h + 1) * 512], p512[pp][:, :]),
                                waits=[("pe", pe_p)] + S_qt.f(), inc=("dv", 1))
                            S_p[pp].consume("dv", w.at("dv"))
                        elif which == "k":
                            w.do("vector", lambda e, h=h, pp=pp: e.tensor_copy(
                                H(KCUR + h * 512, KCUR + (h + 1) * 512), p512[pp][:, :]),
                                waits=[("pe", pe_p)], inc=("dv", 1))
                            dv_k = w.at("dv")
                            S_p[pp].consume("dv", dv_k)
                            w.do("gpsimd", lambda e, h=h, c=c: e.dma_start(
                                out=ks_scr[h, c],
                                in_=H(KCUR + h * 512, KCUR + (h + 1) * 512)),
                                waits=[("dv", dv_k)], inc=("gk", 16))
                        else:
                            w.do("vector", lambda e, pp=pp: e.tensor_copy(
                                harena[:, 31232:31744], p512[pp][:, :]),
                                waits=[("pe", pe_p), ("pe", vstg_pe[0])], inc=("dv", 1))
                            dv_v = w.at("dv")
                            S_p[pp].consume("dv", dv_v)
                            ps = hh % 2

                            def tpv(e, ps=ps):
                                last = None
                                for k in range(4):
                                    last = e.transpose(pt[ps][:, k * 128:(k + 1) * 128],
                                                       harena[:, 31232 + k * 128: 31232 + k * 128 + 128],
                                                       identb[:, :])
                                return last
                            w.do("tensor", tpv, waits=[("dv", dv_v)] + S_pt[ps].f(),
                                 inc=("pe", 1))
                            pv = w.at("pe")
                            vstg_pe[0] = pv
                            S_pt[ps].set_ready("pe", pv)

                            def cpv(e, ps=ps, h=h):
                                o = harena[:, :]
                                dst = bass.AP(o.tensor, o.offset + VCUR + h * 516,
                                              [o.ap[0], [129, 4], [1, 128]])
                                src = pt[ps][:, 0:512].rearrange("p (d q) -> p d q", q=128)
                                return e.tensor_copy(dst, src)
                            w.do("vector", cpv, waits=[("pe", pv)], inc=("dv", 1))
                            dv_vc = w.at("dv")
                            S_pt[ps].consume("dv", dv_vc)

                            def stv(e, h=h, c=c):
                                o = harena[:, :]
                                src = bass.AP(o.tensor, o.offset + VCUR + h * 516,
                                              [o.ap[0], [129, 4], [1, 128]])
                                return e.dma_start(
                                    out=vs_scr[h, c * 4:(c + 1) * 4].rearrange(
                                        "k p e -> p k e"), in_=src)
                            w.do("gpsimd", stv, waits=[("dv", dv_vc)], inc=("gk", 16))
            w.do("gpsimd", lambda e: e.nop(), waits=[("gk", w.at("gk"))], inc=("gkc", 1))
            state["ks_gp"][(seq, c)] = w.at("gkc")
            S_xnT.free = {"pe": w.at("pe")}
            dv_qkv = w.at("dv")
            S_qt.set_ready("dv", dv_qkv)
            pt_pe = state.get("pt_pe", 0)
            obb_pe = state.get("obb_pe", 0)
            for h in range(HB):
                NKT = 4 * (c + 1)
                ksl = h % 2
                if c > 0:
                    prev = state["ks_gp"].get((seq, c - 1), 0)
                    w.do("sync", lambda e, h=h, c=c, ksl=ksl: e.dma_start(
                        out=H(KOLD[ksl], KOLD[ksl] + c * 512).rearrange(
                            "p (c f) -> p c f", f=512),
                        in_=ks_scr[h, 0:c].rearrange("c p f -> p c f")),
                        waits=[("gkc", prev)] + S_kold[ksl].f(), inc=(f"ko{ksl}", 16))
                    S_kold[ksl].set_ready(f"ko{ksl}", w.at(f"ko{ksl}"))

                    def ldv(e, h=h, c=c, ksl=ksl):
                        o = harena[:, :]
                        dst = bass.AP(o.tensor, o.offset + VOLD[ksl],
                                      [o.ap[0], [129, 4 * c], [1, 128]])
                        return e.dma_start(out=dst, in_=vs_scr[h, 0:4 * c].rearrange(
                            "k p e -> p k e"))
                    w.do("sync", ldv, waits=[("gkc", prev), ("dv", w.at("dv"))]
                         + S_vold[ksl].f(), inc=(f"vo{ksl}", 16))
                    S_vold[ksl].set_ready(f"vo{ksl}", w.at(f"vo{ksl}"))
                kold_r = S_kold[ksl].r() if c > 0 else []
                vold_r = S_vold[ksl].r() if c > 0 else []
                for kt in range(NKT):
                    pp = kt % 4
                    if kt < 4 * c:
                        lhs = H(KOLD[ksl] + kt * 128, KOLD[ksl] + kt * 128 + 128)
                    else:
                        lhs = H(KCUR + h * 512 + (kt - 4 * c) * 128,
                                KCUR + h * 512 + (kt - 4 * c) * 128 + 128)
                    w.do("tensor", lambda e, lhs=lhs, h=h, pp=pp: e.matmul(
                        p512[pp][:, :], lhs, aux2[:, h * 512:(h + 1) * 512],
                        start=True, stop=True),
                        waits=S_qt.r() + kold_r + [("dv", dv_qkv)] + S_p[pp].f(),
                        inc=("pe", 1))
                    pe_s = w.at("pe")
                    w.do("scalar", lambda e, kt=kt, pp=pp: e.activation(
                        H(PT0 + kt * 512, PT0 + (kt + 1) * 512), p512[pp][:, :],
                        AF.Exp, bias=biasneg[:, 0:1], scale=INVSQ_DB),
                        waits=[("pe", pe_s), ("pe", pt_pe)], inc=("ac", 1))
                    S_p[pp].consume("ac", w.at("ac"))
                ac_pt = w.at("ac")
                for qt in range(NTT):
                    kt = 4 * c + qt
                    w.do("vector", lambda e, kt=kt, qt=qt: e.tensor_tensor(
                        tmpb[qt][:, 0:128],
                        H(PT0 + kt * 512 + qt * 128, PT0 + kt * 512 + qt * 128 + 128),
                        maskle[:, :], ALU.mult),
                        waits=[("ac", ac_pt), ("ld", state["consts_ld"])] + S_tmpb[qt].f())
                    S_tmpb[qt].set_ready("dv", w.at("dv"))
                for qt in range(NTT):
                    kdiag = 4 * c + qt
                    po = qt % 2

                    def mpv(e, qt=qt, kdiag=kdiag, h=h, po=po, ksl=ksl):
                        last = None
                        for kt in range(kdiag + 1):
                            if kt == kdiag:
                                lhs = tmpb[qt][:, 0:128]
                            else:
                                lhs = H(PT0 + kt * 512 + qt * 128,
                                        PT0 + kt * 512 + qt * 128 + 128)
                            if kt < 4 * c:
                                o = harena[:, :]
                                rhs = bass.AP(o.tensor, o.offset + VOLD[ksl] + kt * 129,
                                              [o.ap[0], [1, 129]])
                            else:
                                o = harena[:, :]
                                rhs = bass.AP(o.tensor,
                                              o.offset + VCUR + h * 516 + (kt - 4 * c) * 129,
                                              [o.ap[0], [1, 129]])
                            last = e.matmul(p1024[:, po * 512: po * 512 + 129], lhs, rhs,
                                            start=(kt == 0), stop=(kt == kdiag))
                        return last
                    w.do("tensor", mpv, waits=[("ac", ac_pt)] + S_tmpb[qt].r() + vold_r
                         + S_p10[po].f(), inc=("pe", 1))
                    pe_pv = w.at("pe")
                    pt_pe = pe_pv
                    state["pt_pe"] = pe_pv
                    S_tmpb[qt].consume("pe", pe_pv)

                    w.do("vector", lambda e, po=po: e.reciprocal(
                        small[:, 61:62], p1024[:, po * 512 + 128: po * 512 + 129]),
                        waits=[("pe", pe_pv), ("ac", state.get("norm_ac", 0))],
                        inc=("dv", 1))
                    dv_r = w.at("dv")
                    w.do("scalar", lambda e, po=po: e.activation(
                        H(OBB[0], OBB[0] + 128), p1024[:, po * 512: po * 512 + 128],
                        AF.Copy, scale=small[:, 61:62]),
                        waits=[("dv", dv_r), ("pe", obb_pe)], inc=("ac", 1))
                    dv_n = w.at("ac")
                    state["norm_ac"] = dv_n
                    S_p10[po].consume("ac", dv_n)
                    ps = (h + qt) % 2
                    w.do("tensor", lambda e, ps=ps: e.transpose(
                        pt[ps][:, 0:128], H(OBB[0], OBB[0] + 128), identb[:, :]),
                        waits=[("ac", dv_n)] + S_pt[ps].f(), inc=("pe", 1))
                    pv = w.at("pe")
                    obb_pe = pv
                    state["obb_pe"] = pv
                    S_pt[ps].set_ready("pe", pv)
                    w.do("vector", lambda e, ps=ps, h=h, qt=qt: e.tensor_copy(
                        aux3[:, h * 512 + qt * 128: h * 512 + qt * 128 + 128],
                        pt[ps][:, 0:128]),
                        waits=[("pe", pv)] + S_ot.f(), inc=("dv", 1))
                    S_pt[ps].consume("dv", w.at("dv"))
                S_kold[ksl].consume("pe", w.at("pe"))
                S_vold[ksl].consume("pe", w.at("pe"))
            S_ot.set_ready("dv", w.at("dv"))
            emit_outproj(sub, base, f_bwo, 16, S_ot, aux3)
            S_ot.free = {"pe": w.at("pe")}
            S_qt.free = {"pe": w.at("pe")}
            S_har.free = {"pe": w.at("pe")}

    with nc.Block() as block:
        @block.sync
        def _(eng):
            w.begin("sync", eng)
            walk()

        @block.scalar
        def _(eng):
            w.begin("scalar", eng)
            walk()

        @block.vector
        def _(eng):
            w.begin("vector", eng)
            walk()

        @block.tensor
        def _(eng):
            w.begin("tensor", eng)
            walk()

        @block.gpsimd
        def _(eng):
            w.begin("gpsimd", eng)
            walk()

    return nc


def _consts():
    ident = np.eye(128)
    identb = ident.astype(np.float32)
    p = np.arange(128)[:, None]
    f = np.arange(128)[None, :]
    maskle = (p <= f).astype(np.float32)
    maskgt = (p > f).astype(np.float32)
    ones = np.ones((128, 128), dtype=np.float32)
    inv = 1.0 / (1e6 ** (np.arange(0, DA, 2, dtype=np.float32) / DA))
    pos = np.arange(S, dtype=np.float32)
    fr = pos[:, None] * inv[None, :]
    cos = np.cos(fr).astype(np.float32)
    sin = np.sin(fr).astype(np.float32)
    import ml_dtypes
    bf = ml_dtypes.bfloat16
    e30 = np.exp(np.float32(-30.0))
    return {
        "c_identb": ident.astype(bf), "c_identf": ident.astype(np.float32),
        "c_maskle": maskle.astype(bf), "c_maskgt": maskgt.astype(bf),
        "c_ones": ones.astype(bf), "c_cos": cos, "c_sin": sin,
        "c_e30le": (maskle * e30).astype(bf), "c_e30gt": (maskgt * e30).astype(bf),
        "c_e30c": (ones * e30).astype(bf),
    }


def kernel(L0_attn_a_hidden_states, L0_attn_a_Wq, L0_attn_a_Wk, L0_attn_a_Wv,
           L0_attn_a_Wo, L0_attn_b_W_Q, L0_attn_b_W_K, L0_attn_b_W_V,
           L0_attn_b_W_O, L0_ffn_w_gate_up, L0_ffn_w_proj):
    x = np.ascontiguousarray(L0_attn_a_hidden_states, dtype=np.float32)
    consts = _consts()
    shared = {
        "wq_a": np.ascontiguousarray(L0_attn_a_Wq, dtype=np.float32),
        "wk_a": np.ascontiguousarray(L0_attn_a_Wk, dtype=np.float32),
        "wv_a": np.ascontiguousarray(L0_attn_a_Wv, dtype=np.float32),
        "wo_a": np.ascontiguousarray(L0_attn_a_Wo, dtype=np.float32),
        "wq_b": np.ascontiguousarray(L0_attn_b_W_Q, dtype=np.float32).reshape(2048, 2048),
        "wk_b": np.ascontiguousarray(L0_attn_b_W_K, dtype=np.float32).reshape(2048, 2048),
        "wv_b": np.ascontiguousarray(L0_attn_b_W_V, dtype=np.float32).reshape(2048, 2048),
        "wo_b": np.ascontiguousarray(L0_attn_b_W_O, dtype=np.float32).reshape(32768, 128),
        "wgu": np.ascontiguousarray(L0_ffn_w_gate_up, dtype=np.float32),
        "wpr": np.ascontiguousarray(L0_ffn_w_proj, dtype=np.float32),
        **consts,
    }
    nc = build(N_SUB)
    in_maps = []
    for i in range(NCORES):
        m = dict(shared)
        m["xin"] = np.ascontiguousarray(x[i * BPC:(i + 1) * BPC].reshape(BPC * S, D))
        in_maps.append(m)
    res = run_bass_kernel_spmd(nc, in_maps, core_ids=list(range(NCORES)))
    outs = [r["out"].reshape(BPC, S, D) for r in res.results]
    return np.ascontiguousarray(np.concatenate(outs, axis=0), dtype=np.float32)


# version 05
# speedup vs baseline: 1.7371x; 1.7371x over previous
import math
import contextlib
import numpy as np
import concourse.bass as bass
import concourse.mybir as mybir
from concourse.bass_utils import run_bass_kernel_spmd

F32 = mybir.dt.float32
BF16 = mybir.dt.bfloat16
AF = mybir.ActivationFunctionType
ALU = mybir.AluOpType
AX = mybir.AxisListType.X

NCORES = 8
B, S, D = 16, 2048, 2048
BPC = B // NCORES
CH = 512
NCH = S // CH
NTT = CH // 128
NDT = D // 128
HA, KVA, DA = 8, 2, 256
HB, DB = 16, 128
HID = 8192
NHT = HID // 128
EPS_LN = 1e-5
EPS_HEAD = 1e-6
EPS_RMS = 1.1920929e-07
THR_HI = 0.5009765625
INVSQ_DB = 1.0 / math.sqrt(128.0)

N_SUB = 4
DEBUG_A = False
DEBUG_TRUNC = False


class Walker:
    def __init__(self, sems):
        self.sems = sems
        self.eng_name = None
        self.eng = None
        self.cnt = None

    def begin(self, eng_name, eng):
        self.eng_name = eng_name
        self.eng = eng
        self.cnt = {k: 0 for k in self.sems}

    CHAIN = {"vector": "dv", "scalar": "ac", "gpsimd": "pl"}

    def do(self, eng_name, fn, waits=(), inc=None):
        own = self.CHAIN.get(eng_name)
        if own is not None:
            waits = list(waits) + [(own, self.cnt[own])]
            if inc is None:
                inc = (own, 1)
        if eng_name == self.eng_name:
            seen = {}
            for s, v in waits:
                if v > 0 and seen.get(s, -1) < v:
                    seen[s] = v
            for s, v in seen.items():
                self.eng.wait_ge(self.sems[s], v)
            last = fn(self.eng)
            if inc is not None:
                assert last is not None, f"task on {eng_name} returned no instruction"
                last.then_inc(self.sems[inc[0]], inc[1])
        if inc is not None:
            self.cnt[inc[0]] += inc[1]

    def at(self, s):
        return self.cnt[s]


class Slot:

    def __init__(self):
        self.ready = {}
        self.free = {}

    def reset(self):
        self.ready = {}
        self.free = {}

    def set_ready(self, sem, val):
        self.ready = {sem: val}

    def consume(self, sem, val):
        self.free[sem] = max(self.free.get(sem, 0), val)

    def r(self):
        return list(self.ready.items())

    def f(self):
        return list(self.free.items())


def build(n_sub=N_SUB):
    nc = bass.Bass("TRN2", target_bir_lowering=False)
    TOK = BPC * S

    xin = nc.dram_tensor("xin", [TOK, D], F32, kind="ExternalInput").ap()
    out = nc.dram_tensor("out", [TOK, D], F32, kind="ExternalOutput").ap()
    dbg = nc.dram_tensor("dbg", [128, 16384], F32, kind="ExternalOutput").ap()
    wq_a = nc.dram_tensor("wq_a", [2048, 2048], F32, kind="ExternalInput").ap()
    wk_a = nc.dram_tensor("wk_a", [512, 2048], F32, kind="ExternalInput").ap()
    wv_a = nc.dram_tensor("wv_a", [512, 2048], F32, kind="ExternalInput").ap()
    wo_a = nc.dram_tensor("wo_a", [2048, 2048], F32, kind="ExternalInput").ap()
    wq_b = nc.dram_tensor("wq_b", [2048, 2048], F32, kind="ExternalInput").ap()
    wk_b = nc.dram_tensor("wk_b", [2048, 2048], F32, kind="ExternalInput").ap()
    wv_b = nc.dram_tensor("wv_b", [2048, 2048], F32, kind="ExternalInput").ap()
    wo_b = nc.dram_tensor("wo_b", [32768, 128], F32, kind="ExternalInput").ap()
    wgu = nc.dram_tensor("wgu", [16384, 2048], F32, kind="ExternalInput").ap()
    wpr = nc.dram_tensor("wpr", [2048, 8192], F32, kind="ExternalInput").ap()
    c_identb = nc.dram_tensor("c_identb", [128, 128], BF16, kind="ExternalInput").ap()
    c_identf = nc.dram_tensor("c_identf", [128, 128], F32, kind="ExternalInput").ap()
    c_maskle = nc.dram_tensor("c_maskle", [128, 128], BF16, kind="ExternalInput").ap()
    c_maskgt = nc.dram_tensor("c_maskgt", [128, 128], BF16, kind="ExternalInput").ap()
    c_ones = nc.dram_tensor("c_ones", [128, 128], BF16, kind="ExternalInput").ap()
    c_cos = nc.dram_tensor("c_cos", [S, 128], F32, kind="ExternalInput").ap()
    c_sin = nc.dram_tensor("c_sin", [S, 128], F32, kind="ExternalInput").ap()
    c_e30le = nc.dram_tensor("c_e30le", [128, 128], BF16, kind="ExternalInput").ap()
    c_e30gt = nc.dram_tensor("c_e30gt", [128, 128], BF16, kind="ExternalInput").ap()
    c_e30c = nc.dram_tensor("c_e30c", [128, 128], BF16, kind="ExternalInput").ap()

    nat_aq = nc.dram_tensor("nat_aq", [2048, 2048], BF16, kind="Internal").ap()
    nat_ak = nc.dram_tensor("nat_ak", [512, 2048], BF16, kind="Internal").ap()
    nat_av = nc.dram_tensor("nat_av", [512, 2048], BF16, kind="Internal").ap()
    nat_ao = nc.dram_tensor("nat_ao", [2048, 2048], BF16, kind="Internal").ap()
    nat_bq = nc.dram_tensor("nat_bq", [2048, 2048], BF16, kind="Internal").ap()
    nat_bk = nc.dram_tensor("nat_bk", [2048, 2048], BF16, kind="Internal").ap()
    nat_bv = nc.dram_tensor("nat_bv", [2048, 2048], BF16, kind="Internal").ap()
    nat_bo = nc.dram_tensor("nat_bo", [32768, 128], BF16, kind="Internal").ap()
    nat_gu = nc.dram_tensor("nat_gu", [16384, 2048], BF16, kind="Internal").ap()
    nat_pr = nc.dram_tensor("nat_pr", [2048, 8192], BF16, kind="Internal").ap()
    f_awq = nc.dram_tensor("f_awq", [4, 16, 128, 512], BF16, kind="Internal").ap()
    f_awk = nc.dram_tensor("f_awk", [1, 16, 128, 512], BF16, kind="Internal").ap()
    f_awv = nc.dram_tensor("f_awv", [1, 16, 128, 512], BF16, kind="Internal").ap()
    f_awo = nc.dram_tensor("f_awo", [16, 4, 128, 512], BF16, kind="Internal").ap()
    f_bwq = nc.dram_tensor("f_bwq", [4, 16, 128, 512], BF16, kind="Internal").ap()
    f_bwk = nc.dram_tensor("f_bwk", [4, 16, 128, 512], BF16, kind="Internal").ap()
    f_bwv = nc.dram_tensor("f_bwv", [4, 16, 128, 512], BF16, kind="Internal").ap()
    f_bwo = nc.dram_tensor("f_bwo", [16, 4, 128, 512], BF16, kind="Internal").ap()
    f_gu = nc.dram_tensor("f_gu", [32, 16, 128, 512], BF16, kind="Internal").ap()
    f_pr = nc.dram_tensor("f_pr", [64, 4, 128, 512], BF16, kind="Internal").ap()
    ks_scr = nc.dram_tensor("ks_scr", [16, 4, 128, 512], BF16, kind="Internal").ap()
    vs_scr = nc.dram_tensor("vs_scr", [16, 16, 128, 128], BF16, kind="Internal").ap()

    ctx = contextlib.ExitStack()

    def sb(name, cols, dt):
        return ctx.enter_context(nc.sbuf_tensor(name, [128, cols], dt))

    xnT = sb("xnT", 16 * 512, BF16)
    xt = [sb(f"xt{i}", 2048, F32) for i in range(2)]
    xnb = sb("xnb", 2048, BF16)
    sqdump = sb("sqdump", 2048, F32)
    outsb = [sb(f"outsb{i}", 512, F32) for i in range(2)]
    xrd = [sb(f"xrd{i}", 512, F32) for i in range(2)]
    identb = sb("identb", 128, BF16)
    identf = sb("identf", 128, F32)
    maskle = sb("maskle", 128, BF16)
    maskgt = sb("maskgt", 128, BF16)
    onesb = sb("onesb", 128, BF16)
    ropeC = sb("ropeC", 128, F32)
    ropeS = sb("ropeS", 128, F32)
    small = sb("small", 64, F32)
    biasneg = sb("biasneg", 1, F32)
    invrms = sb("invrms", 8, F32)
    wsA = sb("wsA", 16 * 512, BF16)
    wsB = sb("wsB", 16 * 512, BF16)
    tmpf1 = sb("tmpf1", 2048, F32)
    tmpb = [sb(f"tmpb{i}", 512, BF16) for i in range(4)]
    e30le = sb("e30le", 128, BF16)
    e30gt = sb("e30gt", 128, BF16)
    c_e30c_sb = sb("c_e30c_sb", 128, BF16)
    sall = sb("sall", 514, F32)
    srowb = sb("srowb", 514, BF16)
    harena = sb("harena", 32768, BF16)
    aux2 = sb("aux2", 8192, BF16)
    aux3 = sb("aux3", 8192, BF16)
    qf32 = sb("qf32", 2048, F32)
    qsm = sb("qsm", 96, F32)

    rem = nc.sbuf_bytes_remaining
    assert rem >= 0, f"SBUF over budget by {-rem} bytes"

    def H(a, b):
        return harena[:, a:b]

    p512 = [ctx.enter_context(nc.psum_tensor(f"p512_{i}", [128, 512], F32)) for i in range(4)]
    p1024 = ctx.enter_context(nc.psum_tensor("p1024", [128, 1024], F32))
    pt = [ctx.enter_context(nc.psum_tensor(f"pt{i}", [128, 512], BF16)) for i in range(2)]

    semnames = ("ld", "sd", "pe", "dv", "ac", "gp", "gc", "gr", "gk", "gkc", "pl",
                "lA", "lB", "lx0", "lx1", "lr0", "lr1", "lt",
                "ko0", "ko1", "vo0", "vo1", "tt0", "tt1", "tt2", "tt3",
                "sd0", "sd1", "sd2", "sd3",
                "gq0", "gq1", "gs0", "gs1", "go0", "go1")
    sems = {s: ctx.enter_context(nc.semaphore(s)) for s in semnames}
    w = Walker(sems)

    S_xt = [Slot(), Slot()]
    S_xnb = Slot()
    S_xrd = [Slot(), Slot()]
    S_out = [Slot(), Slot()]
    S_ws = [Slot(), Slot()]
    S_tmpf = [Slot(), Slot()]
    S_tmpb = [Slot() for _ in range(4)]
    S_p = [Slot() for _ in range(4)]
    S_p10 = [Slot(), Slot()]
    S_pt = [Slot(), Slot()]
    S_xnT = Slot()
    S_qt = Slot()
    S_ot = Slot()
    S_har = Slot()
    S_kold = [Slot(), Slot()]
    S_vold = [Slot(), Slot()]
    ALL_SLOTS = ([S_xnb] + S_xt + S_xrd + S_out + S_ws + S_tmpf + S_tmpb + S_p + S_p10
                 + S_pt + [S_xnT, S_qt, S_ot, S_har] + S_kold + S_vold)

    state = {}

    def walk():
        for sl in ALL_SLOTS:
            sl.reset()
        state.clear()

        for dstt, srct in ((identb, c_identb), (identf, c_identf), (maskle, c_maskle),
                           (maskgt, c_maskgt), (onesb, c_ones), (e30le, c_e30le),
                           (e30gt, c_e30gt), (c_e30c_sb, c_e30c)):
            w.do("sync", lambda e, d=dstt, s=srct: e.dma_start(out=d[:, :], in_=s[:, :]),
                 inc=("ld", 16))
        w.do("vector", lambda e: e.memset(biasneg[:, :], -20.0), inc=("dv", 1))
        consts_ld = w.at("ld")
        state["consts_ld"] = consts_ld

        for src, dst in ((wq_a, nat_aq), (wk_a, nat_ak), (wv_a, nat_av), (wo_a, nat_ao),
                         (wq_b, nat_bq), (wk_b, nat_bk), (wv_b, nat_bv), (wo_b, nat_bo)):
            nr = src.shape[0]
            step = 512
            for r in range(0, nr, step):
                rr = min(step, nr - r)
                w.do("gpsimd", lambda e, s=src, d=dst, r=r, rr=rr: e.dma_start(
                    out=d[r:r + rr, :], in_=s[r:r + rr, :]), inc=("gc", 16))

        cast_gate = [("gc", w.at("gc"))]
        qslot_free = [{}, {}]
        qstore_free = [{}, {}]
        QQ = 20512
        qcnt = [0]

        def quant_tile(qsrc, nat, rt, cc):
            it = qcnt[0]
            qcnt[0] += 1
            CW = 2048
            G = CW // 64
            sl = it % 2
            qbuf = H(QQ + sl * 2048, QQ + sl * 2048 + CW)
            pbuf = H(QQ + 4096 + sl * 2048, QQ + 4096 + sl * 2048 + CW)
            mbuf = H(QQ + 8192, QQ + 8192 + CW)
            scl = H(30816, 30816 + G)
            redf = qsm[:, 0:G]
            thr = qsm[:, 32:32 + G]
            nthr = qsm[:, 64:64 + G]
            w.do("gpsimd", lambda e, s=qsrc, rt=rt, cc=cc, qb=qbuf: e.dma_start(
                out=qb, in_=s[rt * 128:(rt + 1) * 128, cc:cc + CW]),
                waits=list(qslot_free[sl].items()), inc=(f"gq{sl}", 16))
            ldv = (f"gq{sl}", w.at(f"gq{sl}"))
            qs = qbuf.rearrange("p (g k) -> p g k", k=64)
            pbs = pbuf.rearrange("p (g k) -> p g k", k=64)
            mbs = mbuf.rearrange("p (g k) -> p g k", k=64)
            w.do("vector", lambda e, rf=redf, qs=qs: e.tensor_reduce(
                rf, qs, AX, ALU.add, apply_absolute_value=True), waits=[ldv])
            w.do("vector", lambda e, sc=scl, rf=redf: e.tensor_scalar(
                sc, rf, 1.0 / 64.0, 1e-8, ALU.mult, ALU.max))
            w.do("vector", lambda e, th=thr, sc=scl: e.tensor_single_scalar(
                th, sc, THR_HI, ALU.mult))
            w.do("vector", lambda e, nt=nthr, th=thr: e.tensor_single_scalar(
                nt, th, -1.0, ALU.mult))
            w.do("vector", lambda e, pbs=pbs, qs=qs, th=thr, G=G: e.tensor_tensor(
                pbs, qs, th[:, :, None].broadcast_to([128, G, 64]), ALU.is_gt),
                waits=list(qstore_free[sl].items()))
            w.do("vector", lambda e, mbs=mbs, qs=qs, nt=nthr, G=G: e.tensor_tensor(
                mbs, qs, nt[:, :, None].broadcast_to([128, G, 64]), ALU.is_lt))
            w.do("vector", lambda e, pb=pbuf, mb=mbuf: e.tensor_tensor(
                pb, pb, mb, ALU.subtract))
            w.do("vector", lambda e, pbs=pbs, sc=scl, G=G: e.tensor_tensor(
                pbs, pbs, sc[:, :, None].broadcast_to([128, G, 64]), ALU.mult))
            dvv = w.at("dv")
            w.do("gpsimd", lambda e, n=nat, rt=rt, cc=cc, pb=pbuf: e.dma_start(
                out=n[rt * 128:(rt + 1) * 128, cc:cc + CW], in_=pb),
                waits=[("dv", dvv)], inc=(f"gs{sl}", 16))
            qslot_free[sl] = {"dv": dvv}
            qstore_free[sl] = {f"gs{sl}": w.at(f"gs{sl}")}

        q_thunks = []
        for rt in range(16384 // 128):
            q_thunks.append((wgu, nat_gu, rt, 0))
        for rt in range(2048 // 128):
            for cc in range(0, 8192, 2048):
                q_thunks.append((wpr, nat_pr, rt, cc))
        state["q_thunks"] = q_thunks

        def drain_quant(k):
            th = state["q_thunks"]
            for _ in range(min(k, len(th))):
                quant_tile(*th.pop(0))
            if not th:
                state["pre_gates"] = cast_gate + [
                    ("gs0", w.at("gs0")), ("gs1", w.at("gs1"))]
                state["quant_dv"] = w.at("dv")
        state["drain_quant"] = drain_quant

        tstate = [Slot() for _ in range(4)]
        for _ts in tstate:
            _ts.free = {"dv": w.at("dv")}
        tcnt = [0]

        def transform(natsrc, r0, RB, c0, dst_ap, gates=None):
            i = tcnt[0] % 4
            tcnt[0] += 1
            sem = f"tt{i}"
            buf = aux2[:, i * 2048: i * 2048 + RB]
            w.do("sync", lambda e, b=buf: e.dma_start(
                out=b, in_=natsrc[r0:r0 + RB, c0:c0 + 128], transpose=True),
                waits=(gates if gates is not None else state["pre_gates"]) + tstate[i].f(),
                inc=(sem, 16))
            lv = w.at(sem)
            if len(dst_ap.shape) == 3:
                bsrc = buf.rearrange("p (e f) -> p e f", f=dst_ap.shape[2])
            else:
                bsrc = buf
            ssem = f"sd{i}"
            w.do("scalar", lambda e, b=bsrc, d=dst_ap: e.dma_start(out=d, in_=b),
                 waits=[(sem, lv)], inc=(ssem, 16))
            tstate[i].free = {ssem: w.at(ssem)}

        for dt in range(16):
            transform(nat_aq, 0, 2048, dt * 128, f_awq[:, dt].rearrange("e p f -> p e f"), gates=cast_gate)
        for dt in range(16):
            transform(nat_ak, 0, 512, dt * 128, f_awk[0, dt], gates=cast_gate)
            transform(nat_av, 0, 512, dt * 128, f_awv[0, dt], gates=cast_gate)
        for eb in range(16):
            transform(nat_ao, 0, 2048, eb * 128, f_awo[eb].rearrange("s p f -> p s f"), gates=cast_gate)
        for dt in range(16):
            transform(nat_bq, 0, 2048, dt * 128, f_bwq[:, dt].rearrange("e p f -> p e f"), gates=cast_gate)
            transform(nat_bk, 0, 2048, dt * 128, f_bwk[:, dt].rearrange("e p f -> p e f"), gates=cast_gate)
            transform(nat_bv, 0, 2048, dt * 128, f_bwv[:, dt].rearrange("e p f -> p e f"), gates=cast_gate)
        for h in range(16):
            transform(nat_bo, h * 2048, 2048, 0, f_bwo[h].rearrange("s p f -> p s f"), gates=cast_gate)
        state["pre_sd_list"] = [(f"sd{i}", w.at(f"sd{i}")) for i in range(4)]
        ftstate = [Slot(), Slot()]
        ftcnt = [0]

        def ftransform(natsrc, r0, c0, dst_ap):
            i = ftcnt[0] % 2
            ftcnt[0] += 1
            sem = f"tt{i}"
            buf = H(20512 + i * 2048, 20512 + i * 2048 + 2048)
            w.do("sync", lambda e, b=buf: e.dma_start(
                out=b, in_=natsrc[r0:r0 + 2048, c0:c0 + 128], transpose=True),
                waits=state["pre_gates"] + [("dv", state["quant_dv"])] + ftstate[i].f(),
                inc=(sem, 16))
            lv = w.at(sem)
            bsrc = buf.rearrange("p (e f) -> p e f", f=dst_ap.shape[2])
            ssem = f"sd{i}"
            w.do("scalar", lambda e, b=bsrc, d=dst_ap: e.dma_start(out=d, in_=b),
                 waits=[(sem, lv)], inc=(ssem, 16))
            ftstate[i].free = {ssem: w.at(ssem)}

        thunks = []
        for hg4 in range(8):
            for dt in range(16):
                thunks.append((ftransform, (nat_gu, hg4 * 2048, dt * 128,
                               f_gu[hg4 * 4:(hg4 + 1) * 4, dt].rearrange("e p f -> p e f"))))
        for ht in range(64):
            thunks.append((ftransform, (nat_pr, 0, ht * 128,
                           f_pr[ht].rearrange("s p f -> p s f"))))
        state["ft_thunks"] = thunks

        def drain_thunks(k):
            th = state["ft_thunks"]
            for _ in range(min(k, len(th))):
                fn, args = th.pop(0)
                fn(*args)
            state["ffn_sd_list"] = [(f"sd{i}", w.at(f"sd{i}")) for i in range(4)]
        state["drain_thunks"] = drain_thunks
        state["ffn_sd_list"] = state["pre_sd_list"]
        state["resid_gr"] = 0
        S_qt.free = {f"sd{i}": w.at(f"sd{i}") for i in range(4)}

        for sub in range(n_sub):
            kind = ("attn_a", "ffn", "attn_b", "ffn")[sub]
            for seq in range(1 if DEBUG_TRUNC else BPC):
                if kind == "attn_a":
                    emit_attn_a_seq(sub, seq)
                elif kind == "attn_b":
                    emit_attn_b_seq(sub, seq)
                else:
                    emit_ffn_seq(sub, seq)
            w.do("gpsimd", lambda e: e.nop(),
                 waits=[("go0", w.at("go0")), ("go1", w.at("go1"))], inc=("gr", 1))
            state["resid_gr"] = w.at("gr")

        if DEBUG_TRUNC:
            w.do("gpsimd", lambda e: e.dma_start(out=dbg[:, 0:8192], in_=aux3[:, :]),
                 waits=[("dv", w.at("dv")), ("ac", w.at("ac")), ("pe", w.at("pe"))],
                 inc=("gp", 16))
            w.do("gpsimd", lambda e: e.dma_start(
                out=dbg[:, 8192:16384].rearrange("p (a f) -> p a f", f=8192),
                in_=harena[:, 8192:16384].rearrange("p (a f) -> p a f", f=8192)),
                waits=[("gp", w.at("gp"))], inc=("gp", 16))
            w.do("gpsimd", lambda e: e.nop(), waits=[("gp", w.at("gp"))])
        w.do("gpsimd", lambda e: e.nop(),
             waits=[("go0", w.at("go0")), ("go1", w.at("go1")), ("gr", w.at("gr")),
                    ("gk", w.at("gk"))] + [(f"sd{i}", w.at(f"sd{i}")) for i in range(4)])

    def resid_src(sub):
        return xin if sub == 0 else out

    def emit_ln_chunk(sub, seq, c):
        base = seq * S + c * CH
        for tt in range(NTT):
            sl = tt % 2
            xts = xt[sl]
            sem = f"lx{sl}"
            w.do("sync", lambda e, x=xts, o=base + tt * 128, s=sub: e.dma_start(
                out=x[:, :], in_=resid_src(s)[o:o + 128, :]),
                waits=[("gr", state["resid_gr"])] + state["pre_sd_list"] + S_xt[sl].f(),
                inc=(sem, 16))
            lv = w.at(sem)

            w.do("vector", lambda e, x=xts: e.reduce_sum(small[:, 0:1], x[:, :], AX),
                 waits=[(sem, lv)])
            w.do("vector", lambda e: e.tensor_single_scalar(
                small[:, 1:2], small[:, 0:1], -1.0 / 2048.0, ALU.mult))
            w.do("scalar", lambda e, x=xts: e.activation(
                sqdump[:, :], x[:, :], AF.Square, bias=small[:, 1:2],
                accum_out=small[:, 2:3]),
                waits=[("dv", w.at("dv"))], inc=("ac", 1))
            ac_sq_ln = w.at("ac")
            w.do("vector", lambda e: e.tensor_scalar(small[:, 3:4], small[:, 2:3],
                                                     1.0 / 2048.0, EPS_LN, ALU.mult, ALU.add),
                 waits=[("ac", w.at("ac"))], inc=("dv", 1))

            w.do("scalar", lambda e: e.activation(small[:, 4:5], small[:, 3:4], AF.Ln),
                 waits=[("dv", w.at("dv"))])
            w.do("scalar", lambda e: e.activation(small[:, 5:6], small[:, 4:5], AF.Exp,
                                                  scale=-0.5))
            w.do("vector", lambda e, x=xts: e.tensor_scalar(
                xnb[:, :], x[:, :], small[:, 1:2], small[:, 5:6], ALU.add, ALU.mult),
                waits=[("ac", w.at("ac"))] + S_xnb.f(), inc=("dv", 1))
            dv_xn = w.at("dv")
            S_xnb.set_ready("dv", dv_xn)
            S_xt[sl].set_ready("ld", lv)
            S_xt[sl].consume("dv", dv_xn)
            S_xt[sl].consume("ac", ac_sq_ln)
            for bq in range(4):
                ps = bq % 2

                def tp(e, bq=bq, ps=ps):
                    last = None
                    for k in range(4):
                        dtile = bq * 4 + k
                        last = e.transpose(pt[ps][:, k * 128:(k + 1) * 128],
                                           xnb[:, dtile * 128:(dtile + 1) * 128],
                                           identb[:, :])
                    return last
                w.do("tensor", tp, waits=[("dv", dv_xn), ("ld", state["consts_ld"])] + S_pt[ps].f(),
                     inc=("pe", 1))
                pv = w.at("pe")
                S_pt[ps].set_ready("pe", pv)

                def cp(e, bq=bq, ps=ps, tt=tt):
                    o = xnT[:, :].rearrange("p (d q) -> p d q", q=512)
                    dst = bass.AP(o.tensor, o.offset + bq * 4 * 512 + tt * 128,
                                  [o.ap[0], [512, 4], [1, 128]])
                    src = pt[ps][:, 0:512].rearrange("p (d q) -> p d q", q=128)
                    return e.tensor_copy(dst, src)
                w.do("vector", cp, waits=[("pe", pv)] + S_xnT.f(), inc=("dv", 1))
                S_pt[ps].consume("dv", w.at("dv"))
            S_xnb.consume("pe", w.at("pe"))
        S_xnT.set_ready("dv", w.at("dv"))

    def rsqrt_small_emit(dst, srcvar, waits):
        w.do("scalar", lambda e: e.activation(small[:, 6:7], srcvar, AF.Ln), waits=waits)
        w.do("scalar", lambda e: e.activation(dst, small[:, 6:7], AF.Exp, scale=-0.5))

    def emit_ffn_seq(sub, seq):
        w.do("vector", lambda e: e.engine_nop(),
             waits=[("dv", state.get("quant_dv", 0)),
                    ("gs0", w.at("gs0")), ("gs1", w.at("gs1"))], inc=("dv", 1))
        for c in range(NCH):
            emit_ln_chunk(sub, seq, c)
            base = seq * S + c * CH
            pending_ones = [None]

            def emit_pending_ones():
                if pending_ones[0] is None:
                    return
                dv_h2, th, g = pending_ones[0]
                pending_ones[0] = None
                w.do("tensor", lambda e, t=th, g=g: e.matmul(
                    p1024[0:1, 0:512], onesb[:, 0:1], tmpb[t][:, :],
                    start=(g == 0), stop=(g == 63)),
                    waits=[("dv", dv_h2), ("ld", state["consts_ld"])]
                    + (S_p10[0].f() if g == 0 else []),
                    inc=("pe", 1))
                S_tmpb[th].consume("pe", w.at("pe"))
            for j in range(16):
                for half, (slot_sb, slot_st, hgsrc) in enumerate(
                        ((wsA, S_ws[0], j), (wsB, S_ws[1], 16 + j))):
                    sem = "lA" if half == 0 else "lB"
                    w.do("sync", lambda e, sbuf=slot_sb, hg=hgsrc: e.dma_start(
                        out=sbuf[:, :].rearrange("p (d f) -> p d f", f=512), in_=f_gu[hg].rearrange("d p f -> p d f")),
                        waits=state["ffn_sd_list"] + slot_st.f(), inc=(sem, 16))
                    slot_st.set_ready(sem, w.at(sem))
                for u in range(4):
                    g = 4 * j + u
                    pg = p512[0] if g % 2 == 0 else p512[2]
                    pu = p512[1] if g % 2 == 0 else p512[3]
                    ig = 0 if g % 2 == 0 else 2
                    iu = 1 if g % 2 == 0 else 3
                    emit_pending_ones()

                    def mmg(e, u=u, p=pg):
                        last = None
                        for dt in range(16):
                            last = e.matmul(p[:, :], wsA[:, dt * 512 + u * 128: dt * 512 + u * 128 + 128],
                                            xnT[:, dt * 512:(dt + 1) * 512],
                                            start=(dt == 0), stop=(dt == 15))
                        return last
                    w.do("tensor", mmg, waits=S_ws[0].r() + S_xnT.r() + S_p[ig].f(),
                         inc=("pe", 1))
                    pe_g = w.at("pe")

                    def mmu(e, u=u, p=pu):
                        last = None
                        for dt in range(16):
                            last = e.matmul(p[:, :], wsB[:, dt * 512 + u * 128: dt * 512 + u * 128 + 128],
                                            xnT[:, dt * 512:(dt + 1) * 512],
                                            start=(dt == 0), stop=(dt == 15))
                        return last
                    w.do("tensor", mmu, waits=S_ws[1].r() + S_xnT.r() + S_p[iu].f(),
                         inc=("pe", 1))
                    pe_u = w.at("pe")
                    if u == 3:
                        S_ws[0].consume("pe", pe_g)
                        S_ws[1].consume("pe", pe_u)
                    ts = g % 2
                    w.do("scalar", lambda e, p=pg, t=ts: e.activation(
                        tmpb[t][:, :], p[:, :], AF.Silu),
                        waits=[("pe", pe_g)] + S_tmpb[ts].f(), inc=("ac", 1))
                    ac_s = w.at("ac")
                    S_p[ig].consume("ac", ac_s)
                    w.do("vector", lambda e, p=pu, t=ts, g=g: e.tensor_tensor(
                        H(g * 512, g * 512 + 512), tmpb[t][:, :], p[:, :], ALU.mult),
                        waits=[("ac", ac_s), ("pe", pe_u)], inc=("dv", 1))
                    dv_h = w.at("dv")
                    S_p[iu].consume("dv", dv_h)
                    S_tmpb[ts].consume("dv", dv_h)
                    th = 2 + g % 2
                    w.do("vector", lambda e, g=g, t=th: e.tensor_tensor(
                        tmpb[t][:, :], H(g * 512, g * 512 + 512), H(g * 512, g * 512 + 512),
                        ALU.mult),
                        waits=S_tmpb[th].f(), inc=("dv", 1))
                    dv_h2 = w.at("dv")
                    pending_ones[0] = (dv_h2, th, g)
            emit_pending_ones()
            pe_sums = w.at("pe")
            S_xnT.free = {"pe": pe_sums}
            S_har.set_ready("dv", w.at("dv"))
            w.do("vector", lambda e: e.tensor_copy(qf32[0:1, 0:512], p1024[0:1, 0:512]),
                 waits=[("pe", pe_sums)] + S_tmpf[0].f(), inc=("dv", 1))
            dv_sums = w.at("dv")
            S_p10[0].consume("dv", dv_sums)
            for tti in range(NTT):
                w.do("tensor", lambda e, t=tti: e.transpose(
                    p1024[:, 640:768], qf32[:, t * 128:(t + 1) * 128], identf[:, :]),
                    waits=[("dv", dv_sums), ("ld", state["consts_ld"])] + S_p10[1].f(), inc=("pe", 1))
                pe_t = w.at("pe")
                w.do("vector", lambda e: e.tensor_copy(small[:, 7:8], p1024[:, 640:641]),
                     waits=[("pe", pe_t), ("ac", w.at("ac"))])
                w.do("vector", lambda e: e.tensor_scalar(
                    small[:, 8:9], small[:, 7:8], 1.0 / 8192.0, EPS_RMS,
                    ALU.mult, ALU.add))
                S_p10[1].consume("dv", w.at("dv"))
                rsqrt_small_emit(invrms[:, tti:tti + 1], small[:, 8:9],
                                 [("dv", w.at("dv"))])
            ac_inv = w.at("ac")
            S_tmpf[0].consume("pe", w.at("pe"))
            for ds in range(4):
                for htg in range(4):
                    sl = htg % 2
                    slot_sb = wsA if sl == 0 else wsB
                    sem = "lA" if sl == 0 else "lB"
                    w.do("sync", lambda e, sbuf=slot_sb, htg=htg, ds=ds: e.dma_start(
                        out=sbuf[:, :].rearrange("p (t f) -> p t f", f=512),
                        in_=f_pr[htg * 16:(htg + 1) * 16, ds].rearrange("t p f -> p t f")),
                        waits=state["ffn_sd_list"] + S_ws[sl].f(), inc=(sem, 16))
                    S_ws[sl].set_ready(sem, w.at(sem))
                    for tt in range(NTT):
                        def mmp(e, htg=htg, tt=tt, sbuf=slot_sb):
                            last = None
                            for hh in range(16):
                                ht = htg * 16 + hh
                                last = e.matmul(p512[tt][:, :],
                                                H(ht * 512 + tt * 128, ht * 512 + tt * 128 + 128),
                                                sbuf[:, hh * 512:(hh + 1) * 512],
                                                start=(ht == 0), stop=(ht == 63))
                            return last
                        w.do("tensor", mmp, waits=S_ws[sl].r() + S_har.r()
                             + (S_p[tt].f() if htg == 0 else []), inc=("pe", 1))
                    S_ws[sl].consume("pe", w.at("pe"))
                pe_proj = w.at("pe")
                for tt in range(NTT):
                    sl = tt % 2
                    sem = f"lr{sl}"
                    w.do("sync", lambda e, tt=tt, ds=ds, s=sub: e.dma_start(
                        out=xrd[sl][:, :],
                        in_=resid_src(s)[base + tt * 128: base + tt * 128 + 128,
                                         ds * 512:(ds + 1) * 512]),
                        waits=S_xrd[sl].f(), inc=(sem, 16))
                    lv = (sem, w.at(sem))
                    osl = (ds * 4 + tt) % 2
                    w.do("scalar", lambda e, tt=tt, osl=osl: e.activation(
                        sqdump[:, osl * 512: osl * 512 + 512], p512[tt][:, :],
                        AF.Copy, scale=invrms[:, tt:tt + 1]),
                        waits=[("pe", pe_proj), ("ac", ac_inv),
                               ("dv", state.get("ffn_evac_dv", 0))], inc=("ac", 1))
                    ac_o = w.at("ac")
                    w.do("vector", lambda e, sl=sl, osl=osl: e.tensor_tensor(
                        outsb[osl][:, :], sqdump[:, osl * 512: osl * 512 + 512],
                        xrd[sl][:, :], ALU.add),
                        waits=[("ac", ac_o), lv] + S_out[osl].f(), inc=("dv", 1))
                    dv_o = w.at("dv")
                    state["ffn_evac_dv"] = dv_o
                    S_p[tt].consume("ac", ac_o)
                    S_xrd[sl].consume("dv", dv_o)
                    w.do("gpsimd", lambda e, tt=tt, ds=ds, osl=osl: e.dma_start(
                        out=out[base + tt * 128: base + tt * 128 + 128,
                                ds * 512:(ds + 1) * 512],
                        in_=outsb[osl][:, :]),
                        waits=[("dv", dv_o)], inc=(f"go{osl}", 16))
                    S_out[osl].consume(f"go{osl}", w.at(f"go{osl}"))
            S_har.free = {"pe": w.at("pe")}

    def dump(cond, srcap, col0, n, wait_sems, psum=False):
        if not (DEBUG_A == 1 and cond):
            return
        prior = state.get("dump_gp", 0)
        if psum:
            w.do("vector", lambda e: e.tensor_copy(sqdump[:, 0:n], srcap),
                 waits=wait_sems + [("gp", prior)], inc=("dv", 1))
            dvv = w.at("dv")
            w.do("gpsimd", lambda e: e.dma_start(out=dbg[:, col0:col0 + n],
                                                 in_=sqdump[:, 0:n]),
                 waits=[("dv", dvv), ("gp", prior)], inc=("gp", 16))
        else:
            w.do("gpsimd", lambda e: e.dma_start(out=dbg[:, col0:col0 + n], in_=srcap),
                 waits=wait_sems + [("gp", prior)], inc=("gp", 16))
        state["dump_gp"] = w.at("gp")
        w.do("vector", lambda e: e.engine_nop(), waits=[("gp", w.at("gp"))])
        w.do("scalar", lambda e: e.nop(), waits=[("gp", w.at("gp"))], inc=("ac", 1))

    def emit_attn_a_seq(sub, seq):
        KC = lambda kv, et: (kv * 2 + et) * 2048
        VC = lambda kv, blk: 8192 + kv * (16 * 257) + blk * 257
        QB = (16416, 18464)
        KVB = (18464, 18976)
        PA0, PA1 = 18976, 19616
        OB = (20256, 20512)
        nch = DEBUG_TRUNC if DEBUG_TRUNC else NCH

        def vca_ones(e):
            o = harena[:, :]
            ap = bass.AP(o.tensor, o.offset + 8192 + 256,
                         [o.ap[0], [16 * 257, 2], [257, 16], [1, 1]])
            return e.memset(ap, 1.0)
        w.do("vector", vca_ones, waits=S_har.f(), inc=("dv", 1))
        S_har.reset()

        w.do("sync", lambda e: e.dma_start(
            out=wsB[:, :].rearrange("p (d f) -> p d f", f=512),
            in_=f_awv[0].rearrange("d p f -> p d f")),
            waits=state["pre_sd_list"] + S_ws[1].f(), inc=("lB", 16))
        S_ws[1].set_ready("lB", w.at("lB"))
        for c in range(nch):
            emit_ln_chunk(sub, seq, c)
            for tt in range(NTT):
                def mmv(e, tt=tt):
                    last = None
                    for dt in range(16):
                        last = e.matmul(p512[3][:, :],
                                        xnT[:, dt * 512 + tt * 128: dt * 512 + tt * 128 + 128],
                                        wsB[:, dt * 512:(dt + 1) * 512],
                                        start=(dt == 0), stop=(dt == 15))
                    return last
                w.do("tensor", mmv, waits=S_ws[1].r() + S_xnT.r() + S_p[3].f(),
                     inc=("pe", 1))
                pe_v = w.at("pe")
                w.do("vector", lambda e: e.tensor_copy(qf32[:, 512:1024], p512[3][:, :]),
                     waits=[("pe", pe_v)], inc=("dv", 1))
                S_p[3].consume("dv", w.at("dv"))
                emit_headln_only(tt, qf32[:, 512:1024], 2, dst=H(*KVB),
                                 extra_wait=[("dv", state.get("kvb_dv", 0))])

                def cpv(e, tt=tt, c=c):
                    o = harena[:, :]
                    blk = c * 4 + tt
                    dst = bass.AP(o.tensor, o.offset + 8192 + blk * 257,
                                  [o.ap[0], [16 * 257, 2], [1, 256]])
                    src = H(*KVB).rearrange("p (kv f) -> p kv f", f=256)
                    return e.tensor_copy(dst, src)
                w.do("vector", cpv, inc=("dv", 1))
                state["kvb_dv"] = w.at("dv")
            S_xnT.free = {"pe": w.at("pe")}
            if seq == 0:
                state["drain_quant"](12)
            else:
                state["drain_thunks"](24)
        dv_vall = w.at("dv")
        for kv in range(2):
            def msall(e, kv=kv):
                last = None
                for kt in range(16):
                    last = e.matmul(p1024[0:1, 704:961], onesb[:, 0:1],
                                    harena[:, VC(kv, kt): VC(kv, kt) + 257],
                                    start=(kt == 0), stop=(kt == 15))
                return last
            w.do("tensor", msall, waits=[("dv", dv_vall), ("ld", state["consts_ld"]),
                                         ("dv", state.get("scov_dv", 0))]
                 + S_p10[1].f(), inc=("pe", 1))
            pe_s = w.at("pe")
            w.do("vector", lambda e, kv=kv: e.tensor_copy(
                sall[0:1, kv * 257:(kv + 1) * 257], p1024[0:1, 704:961]),
                waits=[("pe", pe_s)], inc=("dv", 1))
            state["scov_dv"] = w.at("dv")
        S_p10[1].consume("dv", w.at("dv"))

        for c in range(nch):
            emit_ln_chunk(sub, seq, c)
            base = seq * S + c * CH
            for tt in range(NTT):
                t0 = c * CH + tt * 128
                w.do("sync", lambda e, t0=t0: e.dma_start(
                    out=ropeC[:, :], in_=c_cos[t0:t0 + 128, :]),
                    waits=[("dv", w.at("dv"))], inc=("lt", 16))
                w.do("sync", lambda e, t0=t0: e.dma_start(
                    out=ropeS[:, :], in_=c_sin[t0:t0 + 128, :]),
                    inc=("lt", 16))
                rope_ld = ("lt", w.at("lt"))
                for es in range(4):
                    sl = es % 2
                    slot_sb = wsA if sl == 0 else wsB
                    sem = "lA" if sl == 0 else "lB"
                    w.do("sync", lambda e, sbuf=slot_sb, es=es: e.dma_start(
                        out=sbuf[:, :].rearrange("p (d f) -> p d f", f=512),
                        in_=f_awq[es].rearrange("d p f -> p d f")),
                        waits=state["pre_sd_list"] + S_ws[sl].f(), inc=(sem, 16))
                    S_ws[sl].set_ready(sem, w.at(sem))
                    pp = es % 2

                    def mmq(e, es=es, tt=tt, sbuf=slot_sb, pp=pp):
                        last = None
                        for dt in range(16):
                            last = e.matmul(p512[pp][:, :],
                                            xnT[:, dt * 512 + tt * 128: dt * 512 + tt * 128 + 128],
                                            sbuf[:, dt * 512:(dt + 1) * 512],
                                            start=(dt == 0), stop=(dt == 15))
                        return last
                    w.do("tensor", mmq, waits=S_ws[sl].r() + S_xnT.r() + S_p[pp].f(),
                         inc=("pe", 1))
                    pe_q = w.at("pe")
                    S_ws[sl].consume("pe", pe_q)
                    w.do("vector", lambda e, es=es, pp=pp: e.tensor_copy(
                        qf32[:, es * 512:(es + 1) * 512], p512[pp][:, :]),
                        waits=[("pe", pe_q)], inc=("dv", 1))
                    S_p[pp].consume("dv", w.at("dv"))
                emit_rope_ln(tt, qf32[:, :], 8, rope_ld, scale16=True, dst=H(*QB),
                             extra_wait=[("pe", state.get("qb_pe", 0))])
                dv_qb = w.at("dv")
                emit_transpose_tiles(H(*QB), 16, dv_qb,
                                     lambda dtile: (dtile * 512 + tt * 128), S_qt, aux2)
                state["qb_pe"] = w.at("pe")
                sl = 0
                w.do("sync", lambda e: e.dma_start(
                    out=wsA[:, :].rearrange("p (d f) -> p d f", f=512),
                    in_=f_awk[0].rearrange("d p f -> p d f")),
                    waits=state["pre_sd_list"] + S_ws[0].f(), inc=("lA", 16))
                S_ws[0].set_ready("lA", w.at("lA"))

                def mmk(e, tt=tt):
                    last = None
                    for dt in range(16):
                        last = e.matmul(p512[2][:, :],
                                        xnT[:, dt * 512 + tt * 128: dt * 512 + tt * 128 + 128],
                                        wsA[:, dt * 512:(dt + 1) * 512],
                                        start=(dt == 0), stop=(dt == 15))
                    return last
                w.do("tensor", mmk, waits=S_ws[0].r() + S_xnT.r() + S_p[2].f(),
                     inc=("pe", 1))
                pe_k = w.at("pe")
                S_ws[0].consume("pe", pe_k)
                w.do("vector", lambda e: e.tensor_copy(qf32[:, 0:512], p512[2][:, :]),
                     waits=[("pe", pe_k)], inc=("dv", 1))
                S_p[2].consume("dv", w.at("dv"))
                emit_rope_ln(tt, qf32[:, 0:512], 2, rope_ld, scale16=False,
                             dst=H(*KVB), extra_wait=[("pe", state.get("kvb_pe", 0))])
                dv_kb = w.at("dv")
                ps = tt % 2

                def tpk(e, ps=ps):
                    last = None
                    for k in range(4):
                        last = e.transpose(pt[ps][:, k * 128:(k + 1) * 128],
                                           H(*KVB)[:, k * 128:(k + 1) * 128],
                                           identb[:, :])
                    return last
                w.do("tensor", tpk, waits=[("dv", dv_kb)] + S_pt[ps].f(), inc=("pe", 1))
                pv = w.at("pe")
                S_pt[ps].set_ready("pe", pv)
                state["kvb_pe"] = pv

                def cpk(e, ps=ps, tt=tt, c=c):
                    o = harena[:, :]
                    dst = bass.AP(o.tensor, o.offset + c * 512 + tt * 128,
                                  [o.ap[0], [2048, 4], [1, 128]])
                    src = pt[ps][:, 0:512].rearrange("p (d q) -> p d q", q=128)
                    return e.tensor_copy(dst, src)
                w.do("vector", cpk, waits=[("pe", pv)], inc=("dv", 1))
                S_pt[ps].consume("dv", w.at("dv"))
            S_xnT.free = {"pe": w.at("pe")}
            dv_kv = w.at("dv")
            pa_pe = state.setdefault("pa_pe", {0: 0, 1: 0})
            for qt in range(NTT):
                Qg = c * 4 + qt
                kts = list(range(max(0, Qg - 4), Qg + 1))
                nk = len(kts)
                for kv in range(2):
                    def mcov(e, kts=kts, kv=kv):
                        last = None
                        for i, kt in enumerate(kts):
                            last = e.matmul(p1024[0:1, 704:961], onesb[:, 0:1],
                                            harena[:, VC(kv, kt): VC(kv, kt) + 257],
                                            start=(i == 0), stop=(i == len(kts) - 1))
                        return last
                    w.do("tensor", mcov, waits=[("dv", state.get("scov_dv", 0))]
                         + [("dv", dv_kv)], inc=("pe", 1))
                    pe_c = w.at("pe")
                    w.do("vector", lambda e, kv=kv: e.tensor_sub(
                        srowb[0:1, kv * 257:(kv + 1) * 257],
                        sall[0:1, kv * 257:(kv + 1) * 257], p1024[0:1, 704:961]),
                        waits=[("pe", pe_c)], inc=("dv", 1))
                    state["scov_dv"] = w.at("dv")
                dv_srow = w.at("dv")
                for h in range(HA):
                    kv = h // 4
                    par = (qt * 8 + h) % 2
                    if par == 0:
                        def mks(e, kts=kts, h=h, qt=qt, kv=kv):
                            last = None
                            for i, kt in enumerate(kts):
                                for et in range(2):
                                    last = e.matmul(
                                        p1024[:, i * 128:(i + 1) * 128],
                                        harena[:, KC(kv, et) + kt * 128: KC(kv, et) + kt * 128 + 128],
                                        aux2[:, (h * 2 + et) * 512 + qt * 128:
                                             (h * 2 + et) * 512 + qt * 128 + 128],
                                        start=(et == 0), stop=(et == 1))
                            return last
                        wl = S_p10[0].f() + S_p10[1].f()
                        w.do("tensor", mks, waits=S_qt.r() + wl
                             + [("dv", dv_kv), ("dv", dv_srow)], inc=("pe", 1))
                        pe_s = w.at("pe")
                        w.do("scalar", lambda e, nk=nk: e.activation(
                            p1024[:, 0:nk * 128], p1024[:, 0:nk * 128],
                            AF.Tanh, scale=1.0 / 30.0),
                            waits=[("pe", pe_s), ("pe", pa_pe[0])])
                        w.do("scalar", lambda e, nk=nk: e.activation(
                            H(PA0, PA0 + nk * 128), p1024[:, 0:nk * 128],
                            AF.Exp, scale=30.0))
                        ac_e = w.at("ac")
                        S_p10[0].consume("ac", ac_e)
                        S_p10[1].consume("ac", ac_e)
                        pa = PA0
                    else:
                        def mks(e, kts=kts, h=h, qt=qt, kv=kv):
                            last = None
                            for i, kt in enumerate(kts):
                                tgt = (p512[0][:, (i % 4) * 128:(i % 4) * 128 + 128] if i < 4
                                       else p512[1][:, 0:128])
                                for et in range(2):
                                    last = e.matmul(
                                        tgt,
                                        harena[:, KC(kv, et) + kt * 128: KC(kv, et) + kt * 128 + 128],
                                        aux2[:, (h * 2 + et) * 512 + qt * 128:
                                             (h * 2 + et) * 512 + qt * 128 + 128],
                                        start=(et == 0), stop=(et == 1))
                            return last
                        w.do("tensor", mks, waits=S_qt.r() + S_p[0].f() + S_p[1].f()
                             + [("dv", dv_kv), ("dv", dv_srow)], inc=("pe", 1))
                        pe_s = w.at("pe")
                        n0 = min(nk, 4)
                        w.do("scalar", lambda e, n0=n0: e.activation(
                            p512[0][:, 0:n0 * 128], p512[0][:, 0:n0 * 128],
                            AF.Tanh, scale=1.0 / 30.0),
                            waits=[("pe", pe_s), ("pe", pa_pe[1])])
                        w.do("scalar", lambda e, n0=n0: e.activation(
                            H(PA1, PA1 + n0 * 128), p512[0][:, 0:n0 * 128],
                            AF.Exp, scale=30.0))
                        if nk == 5:
                            w.do("scalar", lambda e: e.activation(
                                p512[1][:, 0:128], p512[1][:, 0:128],
                                AF.Tanh, scale=1.0 / 30.0))
                            w.do("scalar", lambda e: e.activation(
                                H(PA1 + 512, PA1 + 640), p512[1][:, 0:128],
                                AF.Exp, scale=30.0))
                        ac_e = w.at("ac")
                        S_p[0].consume("ac", ac_e)
                        S_p[1].consume("ac", ac_e)
                        pa = PA1
                    masked = {}
                    def mask_one(i, keepm, e30m, t, pa=pa, ac_e=ac_e):
                        w.do("vector", lambda e, i=i, keepm=keepm, t=t: e.tensor_tensor(
                            tmpb[t][:, 0:128], H(pa + i * 128, pa + i * 128 + 128),
                            keepm[:, :], ALU.mult),
                            waits=[("ac", ac_e), ("ld", state["consts_ld"])]
                            + S_tmpb[t].f())
                        w.do("vector", lambda e, e30m=e30m, t=t: e.tensor_tensor(
                            tmpb[t][:, 0:128], tmpb[t][:, 0:128], e30m[:, :], ALU.add))
                        S_tmpb[t].set_ready("dv", w.at("dv"))
                        masked[i] = t
                    mask_one(nk - 1, maskle, e30gt, 2)
                    if nk == 5:
                        mask_one(0, maskgt, e30le, 3)
                    dv_m = ("dv", w.at("dv"))
                    io = 2 + par

                    def mpv(e, kts=kts, pa=pa, kv=kv, io=io, masked=masked):
                        last = e.matmul(p512[io][:, 0:257], c_e30c_sb[0:1, :],
                                        srowb[0:1, kv * 257:(kv + 1) * 257],
                                        start=True, stop=False)
                        for i, kt in enumerate(kts):
                            lhs = (tmpb[masked[i]][:, 0:128] if i in masked
                                   else H(pa + i * 128, pa + i * 128 + 128))
                            last = e.matmul(p512[io][:, 0:257], lhs,
                                            harena[:, VC(kv, kt): VC(kv, kt) + 257],
                                            start=False, stop=(i == len(kts) - 1))
                        return last
                    w.do("tensor", mpv, waits=[("ac", ac_e), dv_m,
                                               ("dv", dv_srow), ("dv", dv_kv)]
                         + S_p[io].f(), inc=("pe", 1))
                    pe_pv = w.at("pe")
                    pa_pe[par] = pe_pv
                    for i, t in masked.items():
                        S_tmpb[t].consume("pe", pe_pv)
                    w.do("vector", lambda e, io=io: e.reciprocal(
                        small[:, 60:61], p512[io][:, 256:257]),
                        waits=[("pe", pe_pv), ("ac", state.get("norm_ac", 0))],
                        inc=("dv", 1))
                    dv_r = w.at("dv")
                    w.do("scalar", lambda e, io=io: e.activation(
                        H(*OB), p512[io][:, 0:256], AF.Copy, scale=small[:, 60:61]),
                        waits=[("dv", dv_r), ("pe", state.get("ob_pe", 0))],
                        inc=("ac", 1))
                    dv_n = w.at("ac")
                    state["norm_ac"] = dv_n
                    S_p[io].consume("ac", dv_n)
                    ps = (h + qt) % 2

                    def tpo(e, ps=ps):
                        last = None
                        for et in range(2):
                            last = e.transpose(pt[ps][:, et * 128:(et + 1) * 128],
                                               harena[:, OB[0] + et * 128: OB[0] + et * 128 + 128],
                                               identb[:, :])
                        return last
                    w.do("tensor", tpo, waits=[("ac", dv_n)] + S_pt[ps].f(), inc=("pe", 1))
                    pv = w.at("pe")
                    state["ob_pe"] = pv
                    S_pt[ps].set_ready("pe", pv)

                    def cpo(e, ps=ps, h=h, qt=qt):
                        o = aux3[:, :]
                        dst = bass.AP(o.tensor, o.offset + (h * 2) * 512 + qt * 128,
                                      [o.ap[0], [512, 2], [1, 128]])
                        src = pt[ps][:, 0:256].rearrange("p (d q) -> p d q", q=128)
                        return e.tensor_copy(dst, src)
                    w.do("vector", cpo, waits=[("pe", pv)] + S_ot.f(), inc=("dv", 1))
                    S_pt[ps].consume("dv", w.at("dv"))
            S_ot.set_ready("dv", w.at("dv"))
            emit_outproj(sub, base, f_awo, 16, S_ot, aux3)
            S_ot.free = {"pe": w.at("pe")}
            S_qt.free = {"pe": w.at("pe")}
            if seq == 0:
                state["drain_quant"](12)
            else:
                state["drain_thunks"](24)
        if seq == 0:
            state["drain_quant"](10 ** 9)
        else:
            state["drain_thunks"](10 ** 9)
        S_har.free = {"pe": w.at("pe")}

    def emit_rope_ln(tt, srcf32, nheads, rope_ld, scale16, dst, extra_wait=()):
        nh = nheads

        cv = bass.AP(ropeC, 0, [ropeC[:, :].ap[0], [0, nh], [1, 128]])
        sv = bass.AP(ropeS, 0, [ropeS[:, :].ap[0], [0, nh], [1, 128]])
        q1 = srcf32.rearrange("p (h s f) -> p h s f", s=2, f=128)
        q1a = bass.AP(q1.tensor, q1.offset, [q1.ap[0], q1.ap[1], q1.ap[3]])
        q2a = bass.AP(q1.tensor, q1.offset + 128, [q1.ap[0], q1.ap[1], q1.ap[3]])
        t1a = tmpf1[:, 0:nh * 128].rearrange("p (h f) -> p h f", f=128)
        t1b = tmpf1[:, 1024:1024 + nh * 128].rearrange("p (h f) -> p h f", f=128)
        sd = sqdump[:, 0:nh * 256].rearrange("p (h s f) -> p h s f", s=2, f=128)
        o1 = bass.AP(sd.tensor, sd.offset, [sd.ap[0], sd.ap[1], sd.ap[3]])
        o2 = bass.AP(sd.tensor, sd.offset + 128, [sd.ap[0], sd.ap[1], sd.ap[3]])
        w.do("vector", lambda e: e.tensor_tensor(t1a, q2a, sv, ALU.mult),
             waits=[rope_ld] + S_tmpf[1].f())
        w.do("vector", lambda e: e.tensor_tensor(t1b, q1a, cv, ALU.mult))
        w.do("vector", lambda e: e.tensor_tensor(o1, t1b, t1a, ALU.subtract))
        w.do("vector", lambda e: e.tensor_tensor(t1a, q2a, cv, ALU.mult))
        w.do("vector", lambda e: e.tensor_tensor(t1b, q1a, sv, ALU.mult))
        w.do("vector", lambda e: e.tensor_tensor(o2, t1a, t1b, ALU.add))
        S_tmpf[1].consume("dv", w.at("dv"))
        emit_headln_only(tt, sqdump[:, 0:nh * 256], nheads, dst, scale16=scale16,
                         extra_wait=extra_wait)

    def emit_headln_only(tt, srcf32, nheads, dst, scale16=False, extra_wait=()):
        nh = nheads

        svh = srcf32.rearrange("p (h f) -> p h f", f=256)
        w.do("vector", lambda e: e.reduce_sum(small[:, 8:8 + nh], svh, AX))
        w.do("vector", lambda e: e.tensor_single_scalar(
            small[:, 16:16 + nh], small[:, 8:8 + nh], -1.0 / 256.0, ALU.mult))

        def center(e):
            sv = srcf32.rearrange("p (h f) -> p h f", f=256)
            mb = small[:, 16:16 + nh][:, :, None].broadcast_to([128, nh, 256])
            return e.tensor_tensor(qf32[:, 0:nh * 256].rearrange("p (h f) -> p h f", f=256),
                                   sv, mb, ALU.add)
        w.do("vector", center, waits=S_tmpf[0].f(), inc=("dv", 1))
        dv_c = w.at("dv")
        w.do("scalar", lambda e: e.activation(sqdump[:, 0:nh * 256],
                                              qf32[:, 0:nh * 256], AF.Square),
             waits=[("dv", dv_c)], inc=("ac", 1))
        ac_sq = w.at("ac")

        sqv = sqdump[:, 0:nh * 256].rearrange("p (h f) -> p h f", f=256)
        w.do("vector", lambda e: e.reduce_sum(small[:, 24:24 + nh], sqv, AX),
             waits=[("ac", ac_sq)])
        w.do("vector", lambda e: e.tensor_scalar(
            small[:, 32:32 + nh], small[:, 24:24 + nh], 1.0 / 256.0, EPS_HEAD,
            ALU.mult, ALU.add))

        w.do("scalar", lambda e: e.activation(small[:, 40:40 + nh],
                                              small[:, 32:32 + nh], AF.Ln),
             waits=[("dv", w.at("dv"))])
        w.do("scalar", lambda e: e.activation(small[:, 48:48 + nh],
                                              small[:, 40:40 + nh], AF.Exp, scale=-0.5))
        ac_i = w.at("ac")
        if scale16:
            w.do("vector", lambda e: e.tensor_single_scalar(
                small[:, 48:48 + nh], small[:, 48:48 + nh], 16.0, ALU.mult),
                waits=[("ac", ac_i)], inc=("dv", 1))

        def apply(e):
            iv = small[:, 48:48 + nh][:, :, None].broadcast_to([128, nh, 256])
            return e.tensor_tensor(dst.rearrange("p (h f) -> p h f", f=256),
                                   qf32[:, 0:nh * 256].rearrange("p (h f) -> p h f", f=256),
                                   iv, ALU.mult)
        w.do("vector", apply, waits=[("ac", ac_i)] + list(extra_wait), inc=("dv", 1))
        S_tmpf[0].consume("dv", w.at("dv"))

    def emit_transpose_tiles(srcb, ntiles, dv_ready, dstcol, slot, dsttensor):
        for bq in range(ntiles // 4):
            ps = bq % 2

            def tp(e, bq=bq, ps=ps):
                last = None
                for k in range(4):
                    i = bq * 4 + k
                    last = e.transpose(pt[ps][:, k * 128:(k + 1) * 128],
                                       srcb[:, i * 128:(i + 1) * 128], identb[:, :])
                return last
            w.do("tensor", tp, waits=[("dv", dv_ready), ("ld", state["consts_ld"])] + S_pt[ps].f(),
                 inc=("pe", 1))
            pv = w.at("pe")
            S_pt[ps].set_ready("pe", pv)

            def cp(e, bq=bq, ps=ps):
                o = dsttensor[:, :]
                dst = bass.AP(o.tensor, o.offset + dstcol(bq * 4),
                              [o.ap[0], [512, 4], [1, 128]])
                src = pt[ps][:, 0:512].rearrange("p (d q) -> p d q", q=128)
                return e.tensor_copy(dst, src)
            w.do("vector", cp, waits=[("pe", pv)] + slot.f(), inc=("dv", 1))
            S_pt[ps].consume("dv", w.at("dv"))
        slot.set_ready("dv", w.at("dv"))

    def emit_outproj(sub, base, fsrc, nacc, s_ot, ottensor):
        for ds in range(4):
            sl = ds % 2
            slot_sb = wsA if sl == 0 else wsB
            sem = "lA" if sl == 0 else "lB"
            w.do("sync", lambda e, sbuf=slot_sb, ds=ds, f=fsrc: e.dma_start(
                out=sbuf[:, :].rearrange("p (b f) -> p b f", f=512), in_=f[:, ds].rearrange("b p f -> p b f")),
                waits=state["pre_sd_list"] + S_ws[sl].f(), inc=(sem, 16))
            S_ws[sl].set_ready(sem, w.at(sem))
            for qt in range(NTT):
                pp = qt % 2

                def mmo(e, qt=qt, sbuf=slot_sb, pp=pp, nacc=nacc):
                    last = None
                    for bb in range(nacc):
                        last = e.matmul(p512[pp][:, :],
                                        ottensor[:, bb * 512 + qt * 128: bb * 512 + qt * 128 + 128],
                                        sbuf[:, bb * 512:(bb + 1) * 512],
                                        start=(bb == 0), stop=(bb == nacc - 1))
                    return last
                w.do("tensor", mmo, waits=S_ws[sl].r() + s_ot.r() + S_p[pp].f(),
                     inc=("pe", 1))
                pe_o = w.at("pe")
                if qt == NTT - 1:
                    S_ws[sl].consume("pe", pe_o)
                xsl = qt % 2
                sem2 = f"lr{xsl}"
                w.do("sync", lambda e, qt=qt, ds=ds, s=sub: e.dma_start(
                    out=xrd[xsl][:, :],
                    in_=resid_src(s)[base + qt * 128: base + qt * 128 + 128,
                                     ds * 512:(ds + 1) * 512]),
                    waits=S_xrd[xsl].f(), inc=(sem2, 16))
                lv = (sem2, w.at(sem2))
                dbgc = (DEBUG_A == 3 and base == 512 and fsrc is f_awo and ds == 0)
                dump(dbgc and qt == 0, p512[pp][:, :], 0, 512, [("pe", pe_o)], psum=True)
                dump(dbgc and qt == 0, xrd[xsl][:, :], 512, 512, [("ld", lv)])
                dump(dbgc and qt == 2, p512[pp][:, :], 2048, 512, [("pe", pe_o)], psum=True)
                osl = (ds * 4 + qt) % 2
                w.do("vector", lambda e, pp=pp, xsl=xsl, osl=osl: e.tensor_tensor(
                    outsb[osl][:, :], p512[pp][:, :], xrd[xsl][:, :], ALU.add),
                    waits=[("pe", pe_o), lv] + S_out[osl].f(), inc=("dv", 1))
                dv_o = w.at("dv")
                dump(dbgc and qt == 0, outsb[osl][:, :], 1024, 512, [("dv", dv_o)])
                S_p[pp].consume("dv", dv_o)
                S_xrd[xsl].consume("dv", dv_o)
                w.do("gpsimd", lambda e, qt=qt, ds=ds, osl=osl: e.dma_start(
                    out=out[base + qt * 128: base + qt * 128 + 128, ds * 512:(ds + 1) * 512],
                    in_=outsb[osl][:, :]),
                    waits=[("dv", dv_o)], inc=(f"go{osl}", 16))
                S_out[osl].consume(f"go{osl}", w.at(f"go{osl}"))

    def emit_attn_b_seq(sub, seq):
        KCUR = 0
        VCUR = 8192
        KOLD = [16448, 16448 + 1536]
        VOLD = [19520, 19520 + 1548]
        PT0 = 22616
        OBB = (30976, 31232)

        state.setdefault("ks_gp", {})
        for c in range(NCH):
            emit_ln_chunk(sub, seq, c)
            base = seq * S + c * CH
            def ones_b(e):
                o = harena[:, :]
                ap1 = bass.AP(o.tensor, o.offset + VCUR + 128,
                              [o.ap[0], [516, 16], [129, 4], [1, 1]])
                e.memset(ap1, 1.0)
                ap2 = bass.AP(o.tensor, o.offset + VOLD[0] + 128,
                              [o.ap[0], [129, 12], [1, 1]])
                e.memset(ap2, 1.0)
                ap3 = bass.AP(o.tensor, o.offset + VOLD[1] + 128,
                              [o.ap[0], [129, 12], [1, 1]])
                return e.memset(ap3, 1.0)
            w.do("vector", ones_b, waits=S_har.f() + S_vold[0].f() + S_vold[1].f()
                 + [("gkc", w.at("gkc")), ("gr", state["resid_gr"])], inc=("dv", 1))
            S_har.reset()
            vstg_pe = [0]
            for es in range(4):
                for wi, (fsrc, which) in enumerate(((f_bwq, "q"), (f_bwk, "k"), (f_bwv, "v"))):
                    sl = (es * 3 + wi) % 2
                    slot_sb = wsA if sl == 0 else wsB
                    sem = "lA" if sl == 0 else "lB"
                    w.do("sync", lambda e, sbuf=slot_sb, es=es, f=fsrc: e.dma_start(
                        out=sbuf[:, :].rearrange("p (d f) -> p d f", f=512), in_=f[es].rearrange("d p f -> p d f")),
                        waits=state["pre_sd_list"] + S_ws[sl].f(), inc=(sem, 16))
                    S_ws[sl].set_ready(sem, w.at(sem))
                    for hh in range(4):
                        h = es * 4 + hh
                        pp = hh % 2

                        def mmp(e, hh=hh, sbuf=slot_sb, pp=pp):
                            last = None
                            for dt in range(16):
                                last = e.matmul(p512[pp][:, :],
                                                sbuf[:, dt * 512 + hh * 128: dt * 512 + hh * 128 + 128],
                                                xnT[:, dt * 512:(dt + 1) * 512],
                                                start=(dt == 0), stop=(dt == 15))
                            return last
                        w.do("tensor", mmp, waits=S_ws[sl].r() + S_xnT.r() + S_p[pp].f(),
                             inc=("pe", 1))
                        pe_p = w.at("pe")
                        if hh == 3:
                            S_ws[sl].consume("pe", pe_p)
                        if which == "q":
                            w.do("vector", lambda e, h=h, pp=pp: e.tensor_copy(
                                aux2[:, h * 512:(h + 1) * 512], p512[pp][:, :]),
                                waits=[("pe", pe_p)] + S_qt.f(), inc=("dv", 1))
                            S_p[pp].consume("dv", w.at("dv"))
                        elif which == "k":
                            w.do("vector", lambda e, h=h, pp=pp: e.tensor_copy(
                                H(KCUR + h * 512, KCUR + (h + 1) * 512), p512[pp][:, :]),
                                waits=[("pe", pe_p)], inc=("dv", 1))
                            dv_k = w.at("dv")
                            S_p[pp].consume("dv", dv_k)
                            w.do("gpsimd", lambda e, h=h, c=c: e.dma_start(
                                out=ks_scr[h, c],
                                in_=H(KCUR + h * 512, KCUR + (h + 1) * 512)),
                                waits=[("dv", dv_k)], inc=("gk", 16))
                        else:
                            w.do("vector", lambda e, pp=pp: e.tensor_copy(
                                harena[:, 31232:31744], p512[pp][:, :]),
                                waits=[("pe", pe_p), ("pe", vstg_pe[0])], inc=("dv", 1))
                            dv_v = w.at("dv")
                            S_p[pp].consume("dv", dv_v)
                            ps = hh % 2

                            def tpv(e, ps=ps):
                                last = None
                                for k in range(4):
                                    last = e.transpose(pt[ps][:, k * 128:(k + 1) * 128],
                                                       harena[:, 31232 + k * 128: 31232 + k * 128 + 128],
                                                       identb[:, :])
                                return last
                            w.do("tensor", tpv, waits=[("dv", dv_v)] + S_pt[ps].f(),
                                 inc=("pe", 1))
                            pv = w.at("pe")
                            vstg_pe[0] = pv
                            S_pt[ps].set_ready("pe", pv)

                            def cpv(e, ps=ps, h=h):
                                o = harena[:, :]
                                dst = bass.AP(o.tensor, o.offset + VCUR + h * 516,
                                              [o.ap[0], [129, 4], [1, 128]])
                                src = pt[ps][:, 0:512].rearrange("p (d q) -> p d q", q=128)
                                return e.tensor_copy(dst, src)
                            w.do("vector", cpv, waits=[("pe", pv)], inc=("dv", 1))
                            dv_vc = w.at("dv")
                            S_pt[ps].consume("dv", dv_vc)

                            def stv(e, h=h, c=c):
                                o = harena[:, :]
                                src = bass.AP(o.tensor, o.offset + VCUR + h * 516,
                                              [o.ap[0], [129, 4], [1, 128]])
                                return e.dma_start(
                                    out=vs_scr[h, c * 4:(c + 1) * 4].rearrange(
                                        "k p e -> p k e"), in_=src)
                            w.do("gpsimd", stv, waits=[("dv", dv_vc)], inc=("gk", 16))
            w.do("gpsimd", lambda e: e.nop(), waits=[("gk", w.at("gk"))], inc=("gkc", 1))
            state["ks_gp"][(seq, c)] = w.at("gkc")
            S_xnT.free = {"pe": w.at("pe")}
            dv_qkv = w.at("dv")
            S_qt.set_ready("dv", dv_qkv)
            pt_pe = state.get("pt_pe", 0)
            obb_pe = state.get("obb_pe", 0)
            for h in range(HB):
                NKT = 4 * (c + 1)
                ksl = h % 2
                if c > 0:
                    prev = state["ks_gp"].get((seq, c - 1), 0)
                    w.do("sync", lambda e, h=h, c=c, ksl=ksl: e.dma_start(
                        out=H(KOLD[ksl], KOLD[ksl] + c * 512).rearrange(
                            "p (c f) -> p c f", f=512),
                        in_=ks_scr[h, 0:c].rearrange("c p f -> p c f")),
                        waits=[("gkc", prev)] + S_kold[ksl].f(), inc=(f"ko{ksl}", 16))
                    S_kold[ksl].set_ready(f"ko{ksl}", w.at(f"ko{ksl}"))

                    def ldv(e, h=h, c=c, ksl=ksl):
                        o = harena[:, :]
                        dst = bass.AP(o.tensor, o.offset + VOLD[ksl],
                                      [o.ap[0], [129, 4 * c], [1, 128]])
                        return e.dma_start(out=dst, in_=vs_scr[h, 0:4 * c].rearrange(
                            "k p e -> p k e"))
                    w.do("sync", ldv, waits=[("gkc", prev), ("dv", w.at("dv"))]
                         + S_vold[ksl].f(), inc=(f"vo{ksl}", 16))
                    S_vold[ksl].set_ready(f"vo{ksl}", w.at(f"vo{ksl}"))
                kold_r = S_kold[ksl].r() if c > 0 else []
                vold_r = S_vold[ksl].r() if c > 0 else []
                for kt in range(NKT):
                    pp = kt % 4
                    if kt < 4 * c:
                        lhs = H(KOLD[ksl] + kt * 128, KOLD[ksl] + kt * 128 + 128)
                    else:
                        lhs = H(KCUR + h * 512 + (kt - 4 * c) * 128,
                                KCUR + h * 512 + (kt - 4 * c) * 128 + 128)
                    w.do("tensor", lambda e, lhs=lhs, h=h, pp=pp: e.matmul(
                        p512[pp][:, :], lhs, aux2[:, h * 512:(h + 1) * 512],
                        start=True, stop=True),
                        waits=S_qt.r() + kold_r + [("dv", dv_qkv)] + S_p[pp].f(),
                        inc=("pe", 1))
                    pe_s = w.at("pe")
                    w.do("scalar", lambda e, kt=kt, pp=pp: e.activation(
                        H(PT0 + kt * 512, PT0 + (kt + 1) * 512), p512[pp][:, :],
                        AF.Exp, bias=biasneg[:, 0:1], scale=INVSQ_DB),
                        waits=[("pe", pe_s), ("pe", pt_pe)], inc=("ac", 1))
                    S_p[pp].consume("ac", w.at("ac"))
                ac_pt = w.at("ac")
                for qt in range(NTT):
                    kt = 4 * c + qt
                    w.do("vector", lambda e, kt=kt, qt=qt: e.tensor_tensor(
                        tmpb[qt][:, 0:128],
                        H(PT0 + kt * 512 + qt * 128, PT0 + kt * 512 + qt * 128 + 128),
                        maskle[:, :], ALU.mult),
                        waits=[("ac", ac_pt), ("ld", state["consts_ld"])] + S_tmpb[qt].f())
                    S_tmpb[qt].set_ready("dv", w.at("dv"))
                for qt in range(NTT):
                    kdiag = 4 * c + qt
                    po = qt % 2

                    def mpv(e, qt=qt, kdiag=kdiag, h=h, po=po, ksl=ksl):
                        last = None
                        for kt in range(kdiag + 1):
                            if kt == kdiag:
                                lhs = tmpb[qt][:, 0:128]
                            else:
                                lhs = H(PT0 + kt * 512 + qt * 128,
                                        PT0 + kt * 512 + qt * 128 + 128)
                            if kt < 4 * c:
                                o = harena[:, :]
                                rhs = bass.AP(o.tensor, o.offset + VOLD[ksl] + kt * 129,
                                              [o.ap[0], [1, 129]])
                            else:
                                o = harena[:, :]
                                rhs = bass.AP(o.tensor,
                                              o.offset + VCUR + h * 516 + (kt - 4 * c) * 129,
                                              [o.ap[0], [1, 129]])
                            last = e.matmul(p1024[:, po * 512: po * 512 + 129], lhs, rhs,
                                            start=(kt == 0), stop=(kt == kdiag))
                        return last
                    w.do("tensor", mpv, waits=[("ac", ac_pt)] + S_tmpb[qt].r() + vold_r
                         + S_p10[po].f(), inc=("pe", 1))
                    pe_pv = w.at("pe")
                    pt_pe = pe_pv
                    state["pt_pe"] = pe_pv
                    S_tmpb[qt].consume("pe", pe_pv)

                    w.do("vector", lambda e, po=po: e.reciprocal(
                        small[:, 61:62], p1024[:, po * 512 + 128: po * 512 + 129]),
                        waits=[("pe", pe_pv), ("ac", state.get("norm_ac", 0))],
                        inc=("dv", 1))
                    dv_r = w.at("dv")
                    w.do("scalar", lambda e, po=po: e.activation(
                        H(OBB[0], OBB[0] + 128), p1024[:, po * 512: po * 512 + 128],
                        AF.Copy, scale=small[:, 61:62]),
                        waits=[("dv", dv_r), ("pe", obb_pe)], inc=("ac", 1))
                    dv_n = w.at("ac")
                    state["norm_ac"] = dv_n
                    S_p10[po].consume("ac", dv_n)
                    ps = (h + qt) % 2
                    w.do("tensor", lambda e, ps=ps: e.transpose(
                        pt[ps][:, 0:128], H(OBB[0], OBB[0] + 128), identb[:, :]),
                        waits=[("ac", dv_n)] + S_pt[ps].f(), inc=("pe", 1))
                    pv = w.at("pe")
                    obb_pe = pv
                    state["obb_pe"] = pv
                    S_pt[ps].set_ready("pe", pv)
                    w.do("vector", lambda e, ps=ps, h=h, qt=qt: e.tensor_copy(
                        aux3[:, h * 512 + qt * 128: h * 512 + qt * 128 + 128],
                        pt[ps][:, 0:128]),
                        waits=[("pe", pv)] + S_ot.f(), inc=("dv", 1))
                    S_pt[ps].consume("dv", w.at("dv"))
                S_kold[ksl].consume("pe", w.at("pe"))
                S_vold[ksl].consume("pe", w.at("pe"))
            S_ot.set_ready("dv", w.at("dv"))
            emit_outproj(sub, base, f_bwo, 16, S_ot, aux3)
            S_ot.free = {"pe": w.at("pe")}
            S_qt.free = {"pe": w.at("pe")}
            S_har.free = {"pe": w.at("pe")}

    with nc.Block() as block:
        @block.sync
        def _(eng):
            w.begin("sync", eng)
            walk()

        @block.scalar
        def _(eng):
            w.begin("scalar", eng)
            walk()

        @block.vector
        def _(eng):
            w.begin("vector", eng)
            walk()

        @block.tensor
        def _(eng):
            w.begin("tensor", eng)
            walk()

        @block.gpsimd
        def _(eng):
            w.begin("gpsimd", eng)
            walk()

    return nc


def _consts():
    ident = np.eye(128)
    identb = ident.astype(np.float32)
    p = np.arange(128)[:, None]
    f = np.arange(128)[None, :]
    maskle = (p <= f).astype(np.float32)
    maskgt = (p > f).astype(np.float32)
    ones = np.ones((128, 128), dtype=np.float32)
    inv = 1.0 / (1e6 ** (np.arange(0, DA, 2, dtype=np.float32) / DA))
    pos = np.arange(S, dtype=np.float32)
    fr = pos[:, None] * inv[None, :]
    cos = np.cos(fr).astype(np.float32)
    sin = np.sin(fr).astype(np.float32)
    import ml_dtypes
    bf = ml_dtypes.bfloat16
    e30 = np.exp(np.float32(-30.0))
    return {
        "c_identb": ident.astype(bf), "c_identf": ident.astype(np.float32),
        "c_maskle": maskle.astype(bf), "c_maskgt": maskgt.astype(bf),
        "c_ones": ones.astype(bf), "c_cos": cos, "c_sin": sin,
        "c_e30le": (maskle * e30).astype(bf), "c_e30gt": (maskgt * e30).astype(bf),
        "c_e30c": (ones * e30).astype(bf),
    }


def kernel(L0_attn_a_hidden_states, L0_attn_a_Wq, L0_attn_a_Wk, L0_attn_a_Wv,
           L0_attn_a_Wo, L0_attn_b_W_Q, L0_attn_b_W_K, L0_attn_b_W_V,
           L0_attn_b_W_O, L0_ffn_w_gate_up, L0_ffn_w_proj):
    x = np.ascontiguousarray(L0_attn_a_hidden_states, dtype=np.float32)
    consts = _consts()
    shared = {
        "wq_a": np.ascontiguousarray(L0_attn_a_Wq, dtype=np.float32),
        "wk_a": np.ascontiguousarray(L0_attn_a_Wk, dtype=np.float32),
        "wv_a": np.ascontiguousarray(L0_attn_a_Wv, dtype=np.float32),
        "wo_a": np.ascontiguousarray(L0_attn_a_Wo, dtype=np.float32),
        "wq_b": np.ascontiguousarray(L0_attn_b_W_Q, dtype=np.float32).reshape(2048, 2048),
        "wk_b": np.ascontiguousarray(L0_attn_b_W_K, dtype=np.float32).reshape(2048, 2048),
        "wv_b": np.ascontiguousarray(L0_attn_b_W_V, dtype=np.float32).reshape(2048, 2048),
        "wo_b": np.ascontiguousarray(L0_attn_b_W_O, dtype=np.float32).reshape(32768, 128),
        "wgu": np.ascontiguousarray(L0_ffn_w_gate_up, dtype=np.float32),
        "wpr": np.ascontiguousarray(L0_ffn_w_proj, dtype=np.float32),
        **consts,
    }
    nc = build(N_SUB)
    in_maps = []
    for i in range(NCORES):
        m = dict(shared)
        m["xin"] = np.ascontiguousarray(x[i * BPC:(i + 1) * BPC].reshape(BPC * S, D))
        in_maps.append(m)
    res = run_bass_kernel_spmd(nc, in_maps, core_ids=list(range(NCORES)))
    outs = [r["out"].reshape(BPC, S, D) for r in res.results]
    return np.ascontiguousarray(np.concatenate(outs, axis=0), dtype=np.float32)
